# version 06
# baseline (speedup 1.0000x reference)
import contextlib
import numpy as np
import concourse.bass as bass
import concourse.mybir as mybir
from concourse.bass_utils import run_bass_kernel_spmd
from concourse import library_config

F32 = mybir.dt.float32
AF = mybir.ActivationFunctionType
AL = mybir.AluOpType
AX = mybir.AxisListType

D = 2048
HK = 16
NK = 16
TOK = 2048
NCH = 16
SH = TOK + 128
NCH1 = 17
GUARD = 8000

ENGINES = ("sync", "tensor", "vector", "scalar", "gpsimd")


class Walk:

    def __init__(self):
        self.ops = {e: [] for e in ENGINES}
        self.cnt = {}
        self.sems = {}
        self.guard = {}

    def on(self, eng, fn):
        self.ops[eng].append(fn)

    def bump(self, sem, n):
        self.cnt[sem] = self.cnt.get(sem, 0) + n
        return self.cnt[sem]

    def wait(self, eng, sem, val, guard=False):
        if val is None or val <= 0:
            return
        self.on(eng, lambda E, s=sem, v=val: E.wait_ge(self.sems[s], v))
        if guard:
            if eng in self.guard:
                dst, src, gs = self.guard[eng]
                tgt = self.bump(gs, 16)
                self.on(eng, lambda E, dst=dst, src=src, gs=gs:
                        E.dma_start(out=dst, in_=src).then_inc(self.sems[gs], 16))
                self.on(eng, lambda E, gs=gs, v=tgt: E.wait_ge(self.sems[gs], v))
            else:
                def _nops(E):
                    for _ in range(160):
                        E.nop()
                self.on(eng, _nops)

    def dma(self, eng, sem, out_ap, in_ap):
        tgt = self.bump(sem, 16)
        self.on(eng, lambda E, o=out_ap, i=in_ap, s=sem:
                E.dma_start(out=o, in_=i).then_inc(self.sems[s], 16))
        return tgt

    def stage_gate(self, engs=("sync", "tensor", "vector", "scalar")):
        snap = dict(self.cnt)
        for e in engs:
            for sem in ("mm", "ev", "ac", "sd", "ld", "wld", "gd"):
                self.wait(e, sem, snap.get(sem, 0))

    def op(self, eng, sem, fn):
        if sem:
            tgt = self.bump(sem, 1)
            self.on(eng, lambda E, f=fn, s=sem: f(E).then_inc(self.sems[s], 1))
            return tgt
        self.on(eng, fn)
        return None


def stage_inputs(inputs):
    i = inputs
    x = np.ascontiguousarray(
        np.asarray(i["L0_attn_hidden_states"], np.float32).reshape(16384, D))
    Wq = np.asarray(i["L0_attn_Wq"], np.float32)
    Wk = np.asarray(i["L0_attn_Wk"], np.float32)
    Wv = np.asarray(i["L0_attn_Wv"], np.float32)
    Wo = np.asarray(i["L0_attn_Wo"], np.float32)
    Wqc = np.asarray(i["L0_attn_Wq_conv"], np.float32)
    bqc = np.asarray(i["L0_attn_bq_conv"], np.float32)
    Wkc = np.asarray(i["L0_attn_Wk_conv"], np.float32)
    bkc = np.asarray(i["L0_attn_bk_conv"], np.float32)
    Wg = np.asarray(i["L0_attn_Wg"], np.float32)
    bg = np.asarray(i["L0_attn_bg"], np.float32)
    Wb = np.asarray(i["L0_attn_Wb"], np.float32)
    bb = np.asarray(i["L0_attn_bb"], np.float32)
    Cw = np.asarray(i["L0_ffn_classifier_w"], np.float32)
    eb = np.asarray(i["L0_ffn_e_score_correction_bias"], np.float32)
    Gw = np.asarray(i["L0_ffn_gate_w"], np.float32)
    Uw = np.asarray(i["L0_ffn_up_w"], np.float32)
    Dw = np.asarray(i["L0_ffn_down_w"], np.float32)

    def tile_A(W):
        M = W.shape[0] // 128
        return np.ascontiguousarray(
            W.reshape(M, 128, NK, 128).transpose(0, 3, 2, 1).reshape(M, 128, NK * 128))

    wqk_t = tile_A(np.concatenate([Wq, Wk], axis=0))
    wv_t = np.ascontiguousarray(
        Wv.reshape(8, 512, NK, 128).transpose(0, 3, 2, 1).reshape(8, 128, NK * 512))
    wo_t = np.ascontiguousarray(Wo.reshape(D, 32, 128).transpose(1, 2, 0))

    def conv_t(W):
        return np.ascontiguousarray(
            W.reshape(HK, 128, 128, 4).transpose(2, 0, 3, 1).reshape(128, HK * 4 * 128))

    wcq_t = conv_t(Wqc)
    wck_t = conv_t(Wkc)
    bconv_t = np.ascontiguousarray(
        np.concatenate([bqc.reshape(HK, 128).T, bkc.reshape(HK, 128).T], axis=1))
    WGB = np.concatenate([Wg, Wb], axis=0)
    wgb_t = np.ascontiguousarray(
        WGB.T.reshape(NK, 128, 32).transpose(1, 0, 2).reshape(128, NK * 32))
    bgbb_t = np.ascontiguousarray(np.stack([bg, bb], axis=1))
    cw_t = np.ascontiguousarray(
        Cw.T.reshape(NK, 128, 12).transpose(1, 0, 2).reshape(128, NK * 12))
    bias12 = np.ascontiguousarray(np.broadcast_to(eb, (128, 12)))

    def tile_Ae(W3):
        return np.ascontiguousarray(
            W3.reshape(8, 8, 128, NK, 128).transpose(0, 1, 4, 3, 2).reshape(8, 8, 128, NK * 128))

    wgf_t = tile_Ae(Gw)
    wuf_t = tile_Ae(Uw)
    wd_t = np.ascontiguousarray(Dw.reshape(8, 4, 512, 8, 128).transpose(0, 1, 3, 4, 2))
    ident = np.eye(128, dtype=np.float32)
    trimask = np.ascontiguousarray(np.triu(np.ones((128, 128), np.float32)))
    eh = np.zeros((16, 16 * 128), np.float32)
    for h in range(16):
        eh[h, h * 128:(h + 1) * 128] = 1.0

    in_maps = []
    for core in range(8):
        start = core * TOK
        xs = np.zeros((SH, D), np.float32)
        if core % 4 == 0:
            xs[128:] = x[start:start + TOK]
        else:
            xs[:] = x[start - 128:start + TOK]
        r = core % 4
        sel = np.zeros((16, 16), np.float32)
        use = np.zeros((16, 4), np.float32)
        for j in range(4):
            if j < r:
                use[:, j] = 1.0
                for l in range(4):
                    if j < l < r:
                        sel[:, j * 4 + l] = 1.0
        selu = np.ascontiguousarray(np.concatenate([sel, use], axis=1))
        in_maps.append({
            "x_sh": np.ascontiguousarray(xs),
            "wqk_t": wqk_t, "wv_t": wv_t, "wo_t": wo_t,
            "wcq_t": wcq_t, "wck_t": wck_t, "bconv_t": bconv_t,
            "wgb_t": wgb_t, "bgbb_t": bgbb_t, "cw_t": cw_t, "bias12": bias12,
            "wgf_t": wgf_t, "wuf_t": wuf_t, "wd_t": wd_t,
            "ident": ident, "trimask": trimask, "selu": selu, "eh_t": eh,
        })
    return in_maps


SCRATCH = {
    "qkpre_h": [32, 128, SH], "xnfm_h": [NK, 128, TOK], "xntm_h": [NCH, 128, D],
    "qfm_h": [HK, 128, TOK], "kfm_h": [HK, 128, TOK], "v_h": [8, TOK, 512],
    "qhat_h": [NCH, 128, D], "opart_h": [NCH, 128, 2 * D], "mc_h": [NCH, 128, 2 * D],
    "g_h": [NCH, 128, D], "ccin": [128, 4104], "ccout": [4, 128, 4104],
}


def build_program(upto=99, debug=()):
    nc = bass.Bass("TRN2", target_bir_lowering=False, detect_race_conditions=False)
    dt = nc.dram_tensor
    x_sh = dt("x_sh", [SH, D], F32, kind="ExternalInput").ap()
    wqk = dt("wqk_t", [32, 128, NK * 128], F32, kind="ExternalInput").ap()
    wv = dt("wv_t", [8, 128, NK * 512], F32, kind="ExternalInput").ap()
    wo = dt("wo_t", [32, 128, D], F32, kind="ExternalInput").ap()
    wcq = dt("wcq_t", [128, HK * 4 * 128], F32, kind="ExternalInput").ap()
    wck = dt("wck_t", [128, HK * 4 * 128], F32, kind="ExternalInput").ap()
    bconv_h = dt("bconv_t", [128, 32], F32, kind="ExternalInput").ap()
    wgb_h = dt("wgb_t", [128, NK * 32], F32, kind="ExternalInput").ap()
    bgbb_h = dt("bgbb_t", [16, 2], F32, kind="ExternalInput").ap()
    cw_h = dt("cw_t", [128, NK * 12], F32, kind="ExternalInput").ap()
    bias12_h = dt("bias12", [128, 12], F32, kind="ExternalInput").ap()
    wgf = dt("wgf_t", [8, 8, 128, NK * 128], F32, kind="ExternalInput").ap()
    wuf = dt("wuf_t", [8, 8, 128, NK * 128], F32, kind="ExternalInput").ap()
    wd_h = dt("wd_t", [8, 4, 8, 128, 512], F32, kind="ExternalInput").ap()
    ident_h = dt("ident", [128, 128], F32, kind="ExternalInput").ap()
    trimask_h = dt("trimask", [128, 128], F32, kind="ExternalInput").ap()
    selu_h = dt("selu", [16, 20], F32, kind="ExternalInput").ap()
    eh_h = dt("eh_t", [16, NK * 128], F32, kind="ExternalInput").ap()
    y = dt("y", [TOK, D], F32, kind="ExternalOutput").ap()

    hb = {n: dt(n, s, F32) for n, s in SCRATCH.items()}
    hba = {n: h.ap() for n, h in hb.items()}
    dbg = {n: dt("dbg_" + n, SCRATCH[n], F32, kind="ExternalOutput").ap() for n in debug}

    w = Walk()
    sem_names = ["go", "ld", "wld", "mm", "ev", "ac", "pb", "sd", "gd", "cc"] + [f"gu_{e}" for e in ENGINES]

    with contextlib.ExitStack() as outer:
        sems = {n: outer.enter_context(nc.semaphore(n)) for n in sem_names}
        w.sems = sems
        ps = [outer.enter_context(nc.psum_tensor(f"ps{i}", [128, 512], F32))
              for i in range(8)]

        def psb(name, shape):
            return outer.enter_context(nc.sbuf_tensor(name, shape, F32))

        consts = psb("consts", [128, 300])
        junk = psb("junk", [128, 5 * 128])
        c_ident = consts[:, 0:128]
        c_tri = consts[:, 128:256]
        c_bconv = consts[:, 256:288]
        c_b12 = consts[:, 288:300]
        small = psb("small", [16, 24])
        gates_all = psb("gates_all", [16, TOK])
        nlb_all = psb("nlb_all", [16, TOK])
        logits_all = psb("logits_all", [128, NCH * 12])
        ltots = psb("ltots", [16, NCH])
        w_all = psb("w_all", [128, NCH * 12])
        wz_all = psb("wz_all", [128, NCH])
        ehv = psb("ehv", [16, NK * 128])
        for ei, e in enumerate(("sync", "scalar")):
            w.guard[e] = (junk[:, ei * 128:(ei + 1) * 128], ident_h, f"gu_{e}")

        def _clear(E, ss=tuple(sems.values()), go=sems["go"]):
            for s in ss:
                E.sem_clear(s)
            return E.sem_inc(go, 1)
        w.on("gpsimd", _clear)
        w.cnt["go"] = 1
        for eng in ("sync", "tensor", "vector", "scalar"):
            w.wait(eng, "go", 1, guard=True)

        w.dma("sync", "ld", consts[:, 0:128], ident_h)
        w.dma("sync", "ld", consts[:, 128:256], trimask_h)
        w.dma("sync", "ld", consts[:, 256:288], bconv_h)
        w.dma("sync", "ld", consts[:, 288:300], bias12_h)
        w.dma("sync", "ld", small[:, 0:2], bgbb_h)
        w.dma("sync", "ld", ehv[:, :], eh_h)
        t_consts = w.dma("sync", "ld", small[:, 2:22], selu_h)
        w.wait("vector", "ld", t_consts, guard=True)
        w.op("vector", "ev", lambda E: E.tensor_scalar_mul(small[:, 22:23], small[:, 1:2], -1.0))
        negbb = small[:, 22:23]

        def dump(names):
            for n in names:
                if n in dbg:
                    w.wait("scalar", "sd", w.cnt.get("sd", 0))
                    w.wait("scalar", "ev", w.cnt.get("ev", 0))
                    w.wait("scalar", "ac", w.cnt.get("ac", 0))
                    w.wait("scalar", "gd", w.cnt.get("gd", 0))
                    if len(SCRATCH[n]) == 3:
                        for i in range(SCRATCH[n][0]):
                            w.dma("scalar", "gd", dbg[n][i], hba[n][i])
                    else:
                        w.dma("scalar", "gd", dbg[n], hba[n])

        st = contextlib.ExitStack()

        def sbuf(name, shape):
            return st.enter_context(nc.sbuf_tensor(name, shape, F32))

        xst = contextlib.ExitStack()
        xnt = xst.enter_context(nc.sbuf_tensor("xnt", [128, NK * SH], F32))
        xt = sbuf("xt", [128, D])
        xnb = sbuf("xnb", [128, 2, D])
        statb = sbuf("statb", [128, 16])

        a_x = [None] * NCH1
        a_xn = [None] * NCH1
        a_sq = [None] * NCH1
        a_trmm = [None] * NCH1
        a_xnw = [None] * NCH1
        tr_copy_lm = []
        for c in range(NCH1):
            if c >= 1:
                w.wait("sync", "ac", a_sq[c - 1])
                w.wait("sync", "ev", a_xn[c - 1])
            a_x[c] = w.dma("sync", "ld", xt[:, :], x_sh[c * 128:(c + 1) * 128, :])

            xa = xt[:, :]
            stc = statb[:, 8 * (c % 2): 8 * (c % 2) + 8]
            w.wait("scalar", "ld", a_x[c], guard=(c <= 1))
            if c >= 2:
                w.wait("scalar", "ev", a_xn[c - 2])
                w.wait("scalar", "mm", a_trmm[c - 2])
                w.wait("scalar", "sd", a_xnw[c - 2])
            a_sq[c] = w.op("scalar", "ac", lambda E, xa=xa, xnc=xnb[:, c % 2, :], stc=stc:
                           E.activation(xnc, xa, AF.Square))
            w.wait("vector", "ld", a_x[c], guard=(c <= 1))
            w.wait("vector", "ac", a_sq[c])
            lm1 = w.op("vector", "ev", lambda E, xnc=xnb[:, c % 2, :], stc=stc: E.reduce_sum(
                stc[:, 1:2], xnc, axis=AX.X))
            lm2 = w.op("vector", "ev", lambda E, xa=xa, stc=stc: E.reduce_sum(stc[:, 0:1], xa, axis=AX.X))
            w.wait("vector", "ev", lm2)
            lm3 = w.op("vector", "ev", lambda E, stc=stc: E.tensor_scalar(
                stc[:, 2:4], stc[:, 0:2], 1.0 / D, None, op0=AL.mult))
            w.wait("vector", "ev", lm3)
            lm4 = w.op("vector", "ev", lambda E, stc=stc: E.tensor_mul(stc[:, 4:5], stc[:, 2:3], stc[:, 2:3]))
            w.wait("vector", "ev", lm4)
            lm5 = w.op("vector", "ev", lambda E, stc=stc: E.tensor_sub(stc[:, 5:6], stc[:, 3:4], stc[:, 4:5]))
            w.wait("vector", "ev", lm5)
            lm6 = w.op("vector", "ev", lambda E, stc=stc: E.tensor_scalar_add(stc[:, 6:7], stc[:, 5:6], 1e-5))
            w.wait("vector", "ev", lm6)
            lm_rc = w.op("vector", "ev", lambda E, stc=stc: E.reciprocal(stc[:, 7:8], stc[:, 6:7]))
            w.wait("scalar", "ev", lm_rc)
            lm_r = w.op("scalar", "ac", lambda E, stc=stc: E.activation(stc[:, 6:7], stc[:, 7:8], AF.Sqrt))
            w.wait("vector", "ac", lm_r)
            w.wait("vector", "ev", lm3)
            if c >= 2:
                w.wait("vector", "mm", a_trmm[c - 2])
                w.wait("vector", "sd", a_xnw[c - 2])
            xnc = xnb[:, c % 2, :]
            a_xn[c] = w.op("vector", "ev", lambda E, xa=xa, xnc=xnc, stc=stc: E.tensor_scalar(
                xnc, xa, stc[:, 2:3], stc[:, 6:7], op0=AL.subtract, op1=AL.mult))
            if c >= 1:
                w.wait("scalar", "ev", a_xn[c])
                a_xnw[c] = w.dma("scalar", "sd", hba["xntm_h"][c - 1, :, :], xnc)
            for k in range(NK):
                bank = ps[6 + (k % 2)]
                w.wait("tensor", "ev", a_xn[c])
                if len(tr_copy_lm) >= 2:
                    w.wait("tensor", "ev", tr_copy_lm[-2])
                lm_t = w.op("tensor", "mm", lambda E, xnc=xnc, k=k, bank=bank: E.transpose(
                    bank[:, 0:128], xnc[:, k * 128:(k + 1) * 128], c_ident))
                w.wait("vector", "mm", lm_t)
                tr_copy_lm.append(w.op("vector", "ev", lambda E, k=k, c=c, bank=bank:
                                       E.tensor_copy(xnt[:, k * SH + c * 128: k * SH + (c + 1) * 128],
                                                     bank[:, 0:128])))
            a_trmm[c] = w.cnt["mm"]
        lmA = dict(w.cnt)

        for k in range(NK):
            w.wait("scalar", "ev", lmA["ev"])
            w.dma("scalar", "sd", hba["xnfm_h"][k, :, :], xnt[:, k * SH + 128:(k + 1) * SH])

        if upto <= 1:
            dump(["xnfm_h", "xntm_h"])
            return _finish(nc, w, (st, xst), sems)

        w.stage_gate()
        st.close()
        st = contextlib.ExitStack()
        wcol = sbuf("wcol", [128, 2, NK * 128])
        qkstage = sbuf("qkstage", [128, SH])
        CB = [(0, 512), (512, 512), (1024, 512), (1536, 512), (2048, 128)]
        lm_qk_mm = [None] * 32
        lm_qk_w = [None] * 32
        grp_ev = []
        for m in range(32):
            if m >= 2:
                w.wait("sync", "mm", lm_qk_mm[m - 2])
            twld = w.dma("sync", "wld", wcol[:, m % 2, :], wqk[m, :, :])
            w.wait("tensor", "wld", twld, guard=(m <= 1))
            if m == 0:
                w.wait("tensor", "ev", lmA["ev"])
            for gi, (o, wd_) in enumerate(CB):
                bank = ps[len(grp_ev) % 2]
                if len(grp_ev) >= 2:
                    w.wait("tensor", "ev", grp_ev[-2])
                lm = None
                for k in range(NK):
                    lm = w.op("tensor", "mm" if k == NK - 1 else None,
                              lambda E, bank=bank, m=m, k=k, o=o, wd_=wd_: E.matmul(
                                  bank[:, 0:wd_], wcol[:, m % 2, k * 128:(k + 1) * 128],
                                  xnt[:, k * SH + o: k * SH + o + wd_],
                                  start=(k == 0), stop=(k == NK - 1)))
                w.wait("vector", "mm", lm)
                if m >= 1 and gi == 0:
                    w.wait("vector", "sd", lm_qk_w[m - 1])
                grp_ev.append(w.op("vector", "ev", lambda E, bank=bank, m=m, o=o, wd_=wd_:
                                   E.tensor_copy(qkstage[:, o:o + wd_], bank[:, 0:wd_])))
            lm_qk_mm[m] = w.cnt["mm"]
            w.wait("scalar", "ev", grp_ev[-1])
            lm_qk_w[m] = w.dma("scalar", "sd", hba["qkpre_h"][m, :, :], qkstage[:, :])
        lm_qkpre_w = w.cnt["sd"]
        if upto == 2.1:
            return _finish(nc, w, (st, xst), sems)

        w.stage_gate()
        st.close()
        st = contextlib.ExitStack()
        wvb = sbuf("wvb", [128, NK * 512])
        vstage = sbuf("vstage", [128, 512])
        lm_v_mm = [None] * 8
        v_ev, v_w = [], []
        for n in range(8):
            if n >= 1:
                w.wait("sync", "mm", lm_v_mm[n - 1])
            twld = w.dma("sync", "wld", wvb[:, :], wv[n, :, :])
            w.wait("tensor", "wld", twld, guard=(n == 0))
            for tt in range(NCH):
                bank = ps[2 + (tt % 2)]
                if len(v_ev) >= 2:
                    w.wait("tensor", "ev", v_ev[-2])
                lm = None
                for k in range(NK):
                    lm = w.op("tensor", "mm" if k == NK - 1 else None,
                              lambda E, bank=bank, k=k, tt=tt: E.matmul(
                                  bank[:, 0:512],
                                  xnt[:, k * SH + 128 + tt * 128: k * SH + 128 + (tt + 1) * 128],
                                  wvb[:, k * 512:(k + 1) * 512],
                                  start=(k == 0), stop=(k == NK - 1)))
                w.wait("vector", "mm", lm)
                if len(v_w) >= 1:
                    w.wait("vector", "sd", v_w[-1])
                v_ev.append(w.op("vector", "ev", lambda E, bank=bank, tt=tt:
                                 E.tensor_copy(vstage[:, :], bank[:, 0:512])))
                w.wait("scalar", "ev", v_ev[-1])
                v_w.append(w.dma("scalar", "sd", hba["v_h"][n, tt * 128:(tt + 1) * 128, :],
                                 vstage[:, :]))
            lm_v_mm[n] = w.cnt["mm"]
        lm_v_w = w.cnt["sd"]
        if upto == 2.2:
            return _finish(nc, w, (st, xst), sems)

        w.stage_gate()
        st.close()
        st = contextlib.ExitStack()
        wgbb = sbuf("wgbb", [128, NK * 32])
        gbtmp = sbuf("gbtmp", [16, 512])
        t_gb = w.dma("sync", "wld", wgbb[:, :], wgb_h)
        gb_ac = []
        w.wait("tensor", "ev", v_ev[-1])
        for cb in range(4):
            bank_g = ps[4 if cb % 2 == 0 else 2]
            bank_b = ps[5 if cb % 2 == 0 else 3]
            w.wait("tensor", "wld", t_gb, guard=(cb == 0))
            if len(gb_ac) >= 2:
                w.wait("tensor", "ac", gb_ac[-2])
            for k in range(NK):
                w.op("tensor", "mm" if k == NK - 1 else None,
                     lambda E, k=k, cb=cb, bank_g=bank_g: E.matmul(
                         bank_g[0:16, 0:512], wgbb[:, k * 32: k * 32 + 16],
                         xnt[:, k * SH + 128 + cb * 512: k * SH + 128 + (cb + 1) * 512],
                         start=(k == 0), stop=(k == NK - 1)))
            lm_b = None
            for k in range(NK):
                lm_b = w.op("tensor", "mm" if k == NK - 1 else None,
                            lambda E, k=k, cb=cb, bank_b=bank_b: E.matmul(
                                bank_b[0:16, 0:512], wgbb[:, k * 32 + 16: k * 32 + 32],
                                xnt[:, k * SH + 128 + cb * 512: k * SH + 128 + (cb + 1) * 512],
                                start=(k == 0), stop=(k == NK - 1)))
            w.wait("scalar", "mm", lm_b)
            w.op("scalar", "ac", lambda E, cb=cb, bank_g=bank_g: E.activation(
                gates_all[:, cb * 512:(cb + 1) * 512], bank_g[0:16, 0:512],
                AF.Sigmoid, bias=small[:, 0:1]))
            lm_sg2 = w.op("scalar", "ac", lambda E, cb=cb, bank_b=bank_b: E.activation(
                gbtmp[:, :], bank_b[0:16, 0:512], AF.Sigmoid, bias=small[:, 1:2]))
            w.wait("scalar", "ac", lm_sg2)
            lm_ln = w.op("scalar", "ac", lambda E: E.activation(
                gbtmp[:, :], gbtmp[:, :], AF.Ln))
            w.wait("vector", "ac", lm_ln)
            lm_ng = w.op("vector", "ev", lambda E, cb=cb: E.tensor_scalar(
                nlb_all[:, cb * 512:(cb + 1) * 512], gbtmp[:, :], -1.0, 80.0,
                op0=AL.mult, op1=AL.min))
            w.wait("scalar", "ev", lm_ng)
            gb_ac.append(w.op("scalar", "ac", lambda E: E.activation(
                gbtmp[0:1, 0:1], gbtmp[0:1, 0:1], AF.Copy)))

        if upto == 2.3:
            return _finish(nc, w, (st, xst), sems)

        cwb = sbuf("cwb", [128, NK * 12])
        t_cw = w.dma("sync", "wld", cwb[:, :], cw_h)
        rt_ev = []
        for tt in range(NCH):
            bank = ps[6 + (tt % 2)]
            w.wait("tensor", "wld", t_cw, guard=(tt == 0))
            if len(rt_ev) >= 2:
                w.wait("tensor", "ev", rt_ev[-2])
            lm = None
            for k in range(NK):
                lm = w.op("tensor", "mm" if k == NK - 1 else None,
                          lambda E, k=k, tt=tt, bank=bank: E.matmul(
                              bank[:, 0:12],
                              xnt[:, k * SH + 128 + tt * 128: k * SH + 128 + (tt + 1) * 128],
                              cwb[:, k * 12:(k + 1) * 12],
                              start=(k == 0), stop=(k == NK - 1)))
            w.wait("vector", "mm", lm)
            rt_ev.append(w.op("vector", "ev", lambda E, tt=tt, bank=bank: E.tensor_copy(
                logits_all[:, tt * 12:(tt + 1) * 12], bank[:, 0:12])))
        lmB1 = dict(w.cnt)

        if upto <= 2:
            dump(["qkpre_h", "v_h", "xnfm_h", "xntm_h"])
            return _finish(nc, w, (st, xst), sems)

        w.stage_gate()
        st.close()
        xst.close()
        st = contextlib.ExitStack()
        wconv = sbuf("wconv", [128, 2, HK * 4 * 128])
        qksl = sbuf("qksl", [128, HK, 515])
        qkout = sbuf("qkout", [128, HK, 512])
        w.wait("sync", "sd", lm_qkpre_w, guard=True)
        w.dma("sync", "wld", wconv[:, 0, :], wcq)
        t_wc = w.dma("sync", "wld", wconv[:, 1, :], wck)
        conv_mm_lm = None
        conv_last_w = None
        conv_ac = []
        for blk in range(4):
            for qk in range(2):
                if conv_mm_lm is not None:
                    w.wait("sync", "mm", conv_mm_lm)
                off = 128 + blk * 512 - 3
                src = hba["qkpre_h"][qk * 16:(qk + 1) * 16, :, off:off + 515]
                w.dma("sync", "ld", qksl[:, 0:8, :], src[0:8].transpose([1, 0, 2]))
                t_sl = w.dma("sync", "ld", qksl[:, 8:16, :], src[8:16].transpose([1, 0, 2]))
                w.wait("tensor", "wld", t_wc, guard=(blk == 0 and qk == 0))
                w.wait("tensor", "ld", t_sl, guard=(blk == 0 and qk == 0))
                if conv_last_w is not None:
                    w.wait("scalar", "sd", conv_last_w)
                for g in range(HK):
                    bank = ps[g % 2]
                    if len(conv_ac) >= 2:
                        w.wait("tensor", "ac", conv_ac[-2])
                    lm = None
                    for t in range(4):
                        lm = w.op("tensor", "mm" if t == 3 else None,
                                  lambda E, g=g, t=t, qk=qk, bank=bank: E.matmul(
                                      bank[:, 0:512],
                                      wconv[:, qk, (g * 4 + t) * 128:(g * 4 + t + 1) * 128],
                                      qksl[:, g, t:t + 512],
                                      start=(t == 0), stop=(t == 3)))
                    w.wait("scalar", "mm", lm)
                    conv_ac.append(w.op("scalar", "ac", lambda E, g=g, qk=qk, bank=bank:
                                        E.activation(qkout[:, g, :], bank[:, 0:512], AF.Silu,
                                                     bias=c_bconv[:, qk * 16 + g: qk * 16 + g + 1])))
                conv_mm_lm = w.cnt["mm"]
                w.wait("scalar", "ac", conv_ac[-1])
                dst = (hba["qfm_h"] if qk == 0 else hba["kfm_h"])[:, :, blk * 512:(blk + 1) * 512]
                conv_last_w = w.dma("scalar", "sd", dst.transpose([1, 0, 2]), qkout[:, :, :])
        lm_conv_w = w.cnt["sd"]

        if upto <= 3:
            dump(["qfm_h", "kfm_h"])
            return _finish(nc, w, (st, xst), sems)

        w.stage_gate()
        st.close()
        st = contextlib.ExitStack()
        msgZ = sbuf("msgZ", [128, 4104])
        qf = sbuf("qf", [128, HK * 128])
        kf = sbuf("kf", [128, HK * 128])
        vt = sbuf("vt", [128, 4096])
        nlt = sbuf("nlt", [16, 128])
        lg = sbuf("lg", [16, 256])
        LT = sbuf("LT", [128, 16])
        rowt4 = sbuf("rowt4", [1, 16])
        revt = sbuf("revt", [128, 16])
        ehat = sbuf("ehat", [128, 16])
        eltb = sbuf("eltb", [128, 16])
        darg = sbuf("darg", [128, 128])
        dtt = sbuf("dtt", [128, 128])
        aTt = sbuf("aTt", [128, 128])
        egt = sbuf("egt", [128, 128])
        eggt = sbuf("eggt", [128, 128])
        gb_sb = sbuf("gb_sb", [128, 128])
        khatt = sbuf("khatt", [128, 128])
        qhatt = sbuf("qhatt", [128, HK * 128])
        opartt = sbuf("opartt", [128, 2 * D])
        mct = sbuf("mct", [128, 2 * D])

        lm_c_cons = None
        lm_c_w = None
        prev_lm_LT = None
        prev_lm_row4 = None
        prev_lm_elt = None
        w.op("vector", "ev", lambda E: E.memset(msgZ[:, :], 0.0))
        w.wait("sync", "sd", lm_conv_w, guard=True)
        w.wait("sync", "sd", lm_v_w)
        for c in range(1, NCH1):
            cc0 = (c - 1) * 128
            if lm_c_cons is not None:
                w.wait("sync", "mm", lm_c_cons[0])
                w.wait("sync", "ev", lm_c_cons[1])
            w.dma("sync", "ld", qf.ap().rearrange("p (g w) -> p g w", g=HK)[:, 0:8, :],
                  hba["qfm_h"][0:8, :, cc0:cc0 + 128].transpose([1, 0, 2]))
            t_qf = w.dma("sync", "ld", qf.ap().rearrange("p (g w) -> p g w", g=HK)[:, 8:16, :],
                         hba["qfm_h"][8:16, :, cc0:cc0 + 128].transpose([1, 0, 2]))
            w.dma("sync", "ld", kf.ap().rearrange("p (g w) -> p g w", g=HK)[:, 0:8, :],
                  hba["kfm_h"][0:8, :, cc0:cc0 + 128].transpose([1, 0, 2]))
            t_kf = w.dma("sync", "ld", kf.ap().rearrange("p (g w) -> p g w", g=HK)[:, 8:16, :],
                         hba["kfm_h"][8:16, :, cc0:cc0 + 128].transpose([1, 0, 2]))
            t_vt = w.dma("sync", "ld", vt.ap().rearrange("p (n w) -> p n w", n=8),
                         hba["v_h"][:, cc0:cc0 + 128, :].transpose([1, 0, 2]))

            if c == 1:
                w.wait("vector", "ac", lmB1["ac"])
            if lm_c_w is not None:
                w.wait("vector", "sd", lm_c_w)
            lm_nl = w.op("vector", "ev", lambda E, cc0=cc0: E.tensor_tensor_scan(
                nlt[:, :], nlb_all[:, cc0:cc0 + 128], nlb_all[:, cc0:cc0 + 128],
                0.0, op0=AL.add, op1=AL.bypass))
            w.wait("vector", "ev", lm_nl)
            w.op("vector", None, lambda E: E.tensor_scalar_mul(lg[:, 0:128], nlt[:, :], -1.0))
            w.op("vector", None, lambda E, cc0=cc0: E.tensor_copy(
                lg[:, 128:256], gates_all[:, cc0:cc0 + 128]))
            w.op("vector", None, lambda E, c=c: E.tensor_copy(ltots[:, c - 1:c], nlt[:, 127:128]))
            lm_lg = w.op("vector", "ev", lambda E: E.tensor_add(
                msgZ[0:16, 4096:4097], msgZ[0:16, 4096:4097], nlt[:, 127:128]))
            w.wait("tensor", "ev", lm_lg)
            if prev_lm_LT is not None:
                w.wait("tensor", "ev", prev_lm_LT)
            lm_ltr = w.op("tensor", "mm", lambda E: E.transpose(ps[6][0:128, 0:16], nlt[:, :], c_ident[0:16, 0:16]))
            w.wait("vector", "mm", lm_ltr)
            lm_LT = w.op("vector", "ev", lambda E: E.tensor_copy(LT[:, :], ps[6][0:128, 0:16]))
            w.wait("vector", "ev", lm_LT)
            prev_lm_LT = lm_LT
            if prev_lm_row4 is not None:
                w.wait("tensor", "ev", prev_lm_row4)
            if prev_lm_elt is not None:
                w.wait("tensor", "ac", prev_lm_elt)
            lm_t2 = w.op("tensor", "mm", lambda E: E.transpose(
                ps[7][0:1, 0:16], nlt[:, 127:128], c_ident[0:16, 0:16]))
            w.wait("vector", "mm", lm_t2)
            lm_row4 = w.op("vector", "ev", lambda E: E.tensor_copy(rowt4[:, :], ps[7][0:1, 0:16]))
            prev_lm_row4 = lm_row4
            w.wait("tensor", "ev", lm_row4)
            lm_pb0 = w.op("tensor", "mm", lambda E: E.matmul(
                ps[7][:, 16:32], c_tri[0:1, 0:128], rowt4[0:1, 0:16], start=True, stop=True))
            w.wait("vector", "mm", lm_pb0)
            w.wait("vector", "ev", prev_lm_row4)
            lm_rev = w.op("vector", "ev", lambda E: E.tensor_sub(revt[:, :], LT[:, :], ps[7][:, 16:32]))
            w.wait("scalar", "ev", lm_rev)
            lm_ehat = w.op("scalar", "ac", lambda E: E.activation(ehat[:, :], revt[:, :], AF.Exp))
            lm_elt = w.op("scalar", "ac", lambda E: E.activation(eltb[:, :], ps[7][:, 16:32], AF.Exp, scale=-1.0))
            prev_lm_elt = lm_elt

            lm_aT = None
            lm_op1 = None
            lm_kh = None
            lm_z = None
            lm_eg = None
            for h in (range(0) if upto == 4.1 else range(HK)):
                if lm_op1 is not None:
                    w.wait("tensor", "ev", lm_op1)
                if lm_eg is not None:
                    w.wait("tensor", "ac", lm_eg)
                lm_bc = w.op("tensor", "mm", lambda E, h=h: E.matmul(
                    ps[5][:, 0:256], ehv[:, h * 128:(h + 1) * 128], lg[:, 0:256],
                    start=True, stop=True))
                if h == 0:
                    w.wait("tensor", "ld", t_qf, guard=(c == 1))
                    w.wait("tensor", "ld", t_kf)
                    w.wait("tensor", "ld", t_vt)
                if lm_aT is not None:
                    w.wait("tensor", "ev", lm_aT)
                lm_a = w.op("tensor", "mm", lambda E, h=h: E.matmul(
                    ps[0][:, 0:128], kf[:, h * 128:(h + 1) * 128], qf[:, h * 128:(h + 1) * 128],
                    start=True, stop=True))
                w.wait("vector", "mm", lm_bc)
                lm_gbc = w.op("vector", "ev", lambda E: E.tensor_copy(gb_sb[:, :], ps[5][:, 128:256]))
                lm_darg = w.op("vector", "ev", lambda E, h=h: E.tensor_scalar(
                    darg[:, :], ps[5][:, 0:128], LT[:, h:h + 1], 0.0, op0=AL.add, op1=AL.min))
                w.wait("scalar", "ac", lm_elt)
                w.wait("scalar", "ev", lm_darg)
                lm_dt = w.op("scalar", "ac", lambda E: E.activation(dtt[:, :], darg[:, :], AF.Exp))
                w.wait("vector", "ac", lm_dt)
                lm_dm = w.op("vector", "ev", lambda E: E.tensor_mul(dtt[:, :], dtt[:, :], c_tri))
                w.wait("vector", "mm", lm_a)
                w.wait("vector", "ev", lm_dm)
                lm_aT = w.op("vector", "ev", lambda E: E.tensor_mul(aTt[:, :], ps[0][:, 0:128], dtt[:, :]))
                w.wait("tensor", "ev", lm_aT)
                if lm_op1 is not None:
                    w.wait("tensor", "ev", lm_op1)
                w.op("tensor", None, lambda E, h=h: E.matmul(
                    ps[1][:, 0:128], vt[:, h * 256: h * 256 + 128], aTt[:, :],
                    start=True, stop=True))
                lm_oi1 = w.op("tensor", "mm", lambda E, h=h: E.matmul(
                    ps[4][:, 0:128], vt[:, h * 256 + 128: h * 256 + 256], aTt[:, :],
                    start=True, stop=True))
                w.wait("scalar", "mm", lm_bc)
                lm_eg = w.op("scalar", "ac", lambda E: E.activation(egt[:, :], ps[5][:, 0:128], AF.Exp))
                w.wait("vector", "ac", lm_eg)
                w.wait("vector", "ev", lm_gbc)
                lm_egg = w.op("vector", "ev", lambda E: E.tensor_mul(eggt[:, :], egt[:, :], gb_sb[:, :]))
                w.wait("vector", "ev", lm_egg)
                lm_qh = w.op("vector", "ev", lambda E, h=h: E.tensor_mul(
                    qhatt[:, h * 128:(h + 1) * 128], qf[:, h * 128:(h + 1) * 128], eggt[:, :]))
                w.wait("vector", "mm", lm_oi1)
                w.wait("vector", "ev", lm_gbc)
                w.op("vector", None, lambda E, h=h: E.tensor_tensor(
                    opartt[:, h * 256: h * 256 + 128], ps[1][:, 0:128], gb_sb[:, :], op=AL.mult))
                lm_op1 = w.op("vector", "ev", lambda E, h=h: E.tensor_tensor(
                    opartt[:, h * 256 + 128: h * 256 + 256], ps[4][:, 0:128], gb_sb[:, :],
                    op=AL.mult))
                if upto == 4.2:
                    lm_z = lm_op1
                    continue
                if lm_kh is not None:
                    w.wait("tensor", "ac", lm_kh)
                lm_kt = w.op("tensor", "mm", lambda E, h=h: E.transpose(
                    ps[2][:, 0:128], kf[:, h * 128:(h + 1) * 128], c_ident))
                w.wait("scalar", "mm", lm_kt)
                if h == 0:
                    w.wait("scalar", "ac", lm_ehat)
                lm_kh = w.op("scalar", "ac", lambda E, h=h: E.activation(
                    khatt[:, :], ps[2][:, 0:128], AF.Copy, scale=ehat[:, h:h + 1]))
                w.wait("tensor", "ac", lm_kh)
                if lm_z is not None:
                    w.wait("tensor", "ev", lm_z)
                lm_s = w.op("tensor", "mm", lambda E, h=h: E.matmul(
                    ps[3][:, 0:256], khatt[:, :], vt[:, h * 256:(h + 1) * 256],
                    start=True, stop=True))
                w.wait("vector", "mm", lm_s)
                if h == 0:
                    w.wait("vector", "ac", lm_elt)
                w.op("vector", None, lambda E, h=h: E.tensor_copy(
                    mct[:, h * 256:(h + 1) * 256], ps[3][:, 0:256]))
                lm_z = w.op("vector", "ev", lambda E, h=h: E.scalar_tensor_tensor(
                    msgZ[:, h * 256:(h + 1) * 256], msgZ[:, h * 256:(h + 1) * 256],
                    eltb[:, h:h + 1], ps[3][:, 0:256], op0=AL.mult, op1=AL.add))
            lm_c_cons = (w.cnt["mm"], w.cnt["ev"])
            if upto == 4.1:
                continue
            w.wait("scalar", "ev", lm_z)
            w.dma("scalar", "sd", hba["qhat_h"][c - 1, :, :], qhatt[:, :])
            w.dma("scalar", "sd", hba["opart_h"][c - 1, :, :], opartt[:, :])
            lm_c_w = w.dma("scalar", "sd", hba["mc_h"][c - 1, :, :], mct[:, :])

        if upto in (4.1, 4.2, 4.3):
            return _finish(nc, w, (st, xst), sems)
        w.wait("scalar", "ev", w.cnt["ev"])
        t_ccin = w.dma("scalar", "gd", hba["ccin"][:, :], msgZ[:, :])
        w.wait("gpsimd", "gd", t_ccin)
        if upto == 4.4:
            return _finish(nc, w, (st, xst), sems)
        w.op("gpsimd", "cc", lambda E: E.collective_compute(
            "AllGather", AL.bypass, replica_groups=[[0, 1, 2, 3], [4, 5, 6, 7]],
            ins=[hb["ccin"].ap().opt()], outs=[hb["ccout"].ap().opt()]))

        if upto <= 4:
            dump(["qhat_h", "opart_h", "mc_h", "ccout"])
            return _finish(nc, w, (st, xst), sems)

        w.stage_gate()
        st.close()
        st = contextlib.ExitStack()
        rb = sbuf("rb", [128, 8])
        probs_t = sbuf("probs_t", [128, 12])
        bi_t = sbuf("bi_t", [128, 12])
        m1t = sbuf("m1t", [128, 12])
        m2t = sbuf("m2t", [128, 12])
        b2t = sbuf("b2t", [128, 12])
        for tt in range(NCH):
            sl = logits_all[:, tt * 12:(tt + 1) * 12]
            lm_rm = w.op("vector", "ev", lambda E, sl=sl: E.reduce_max(
                rb[:, 0:1], sl, axis=AX.X, negate=True))
            w.wait("scalar", "ev", lm_rm)
            lm_e = w.op("scalar", "ac", lambda E, sl=sl: E.activation(
                probs_t[:, :], sl, AF.Exp, bias=rb[:, 0:1]))
            w.wait("vector", "ac", lm_e)
            lmv = w.op("vector", "ev", lambda E: E.reduce_sum(rb[:, 1:2], probs_t[:, :], axis=AX.X))
            w.wait("vector", "ev", lmv)
            lmv = w.op("vector", "ev", lambda E: E.reciprocal(rb[:, 2:3], rb[:, 1:2]))
            w.wait("vector", "ev", lmv)
            lmv = w.op("vector", "ev", lambda E: E.tensor_scalar_mul(probs_t[:, :], probs_t[:, :], rb[:, 2:3]))
            w.wait("vector", "ev", lmv)
            lmv = w.op("vector", "ev", lambda E: E.tensor_add(bi_t[:, :], probs_t[:, :], c_b12))
            w.wait("vector", "ev", lmv)
            lmv = w.op("vector", "ev", lambda E: E.reduce_max(rb[:, 3:4], bi_t[:, :], axis=AX.X))
            w.wait("vector", "ev", lmv)
            lmv = w.op("vector", "ev", lambda E: E.tensor_scalar(
                m1t[:, :], bi_t[:, :], rb[:, 3:4], None, op0=AL.is_ge))
            w.wait("vector", "ev", lmv)
            lmv = w.op("vector", "ev", lambda E: E.scalar_tensor_tensor(
                b2t[:, :], m1t[:, :], -1e30, bi_t[:, :], op0=AL.mult, op1=AL.add))
            w.wait("vector", "ev", lmv)
            lmv = w.op("vector", "ev", lambda E: E.reduce_max(rb[:, 4:5], b2t[:, :], axis=AX.X))
            w.wait("vector", "ev", lmv)
            lmv = w.op("vector", "ev", lambda E: E.tensor_scalar(
                m2t[:, :], b2t[:, :], rb[:, 4:5], None, op0=AL.is_ge))
            w.wait("vector", "ev", lmv)
            lmv = w.op("vector", "ev", lambda E: E.tensor_add(m1t[:, :], m1t[:, :], m2t[:, :]))
            w.wait("vector", "ev", lmv)
            lmv = w.op("vector", "ev", lambda E: E.tensor_mul(m1t[:, :], m1t[:, :], probs_t[:, :]))
            w.wait("vector", "ev", lmv)
            lmv = w.op("vector", "ev", lambda E, tt=tt: E.tensor_scalar_mul(
                w_all[:, tt * 12:(tt + 1) * 12], m1t[:, :], 1.5))
            w.wait("vector", "ev", lmv)
            w.op("vector", "ev", lambda E, tt=tt: E.reduce_sum(
                wz_all[:, tt:tt + 1], w_all[:, tt * 12 + 8: tt * 12 + 12], axis=AX.X))

        xnf = sbuf("xnf", [128, NK, 512])
        wgu = sbuf("wgu", [128, 4, NK * 128])
        sgt = sbuf("sgt", [128, 512])
        act = sbuf("act", [128, 8, 512])
        wdq = sbuf("wdq", [128, 2, 8 * 512])
        sffn = sbuf("sffn", [128, 4, D])
        gt1 = sbuf("gt1", [128, D])
        xnti = sbuf("xnti", [128, D])

        lm_ss_mm = None
        lm_gu_mm = [None] * 8
        lm_act_ev = [None] * 8
        lm_gu_ac = [None] * 8
        lm_dn_mm = None
        lm_dn_ev = None
        lm_g_w = None
        lm_act_prev = None
        for ss in range(4):
            if lm_ss_mm is not None:
                w.wait("sync", "mm", lm_ss_mm)
            t_xnf = None
            for k in range(NK):
                t_xnf = w.dma("sync", "ld", xnf[:, k, :],
                              hba["xnfm_h"][k, :, ss * 512:(ss + 1) * 512])
            for e in range(8):
                for m in range(8):
                    if m >= 2:
                        w.wait("sync", "mm", lm_gu_mm[m - 2])
                    w.dma("sync", "wld", wgu[:, (m % 2) * 2, :], wgf[e, m, :, :])
                    t_wgu = w.dma("sync", "wld", wgu[:, (m % 2) * 2 + 1, :], wuf[e, m, :, :])
                    bank_g = ps[0 + (m % 2)]
                    bank_u = ps[2 + (m % 2)]
                    w.wait("tensor", "wld", t_wgu, guard=(ss == 0 and e == 0 and m == 0))
                    if e == 0 and m == 0:
                        w.wait("tensor", "ld", t_xnf)
                    if lm_gu_ac[m] is not None:
                        w.wait("tensor", "ac", lm_gu_ac[m][0])
                        w.wait("tensor", "ev", lm_gu_ac[m][1])
                    for k in range(NK):
                        w.op("tensor", None, lambda E, k=k, m=m, bank_g=bank_g: E.matmul(
                            bank_g[:, 0:512], wgu[:, (m % 2) * 2, k * 128:(k + 1) * 128],
                            xnf[:, k, :], start=(k == 0), stop=(k == NK - 1)))
                    for k in range(NK):
                        w.op("tensor", "mm" if k == NK - 1 else None,
                             lambda E, k=k, m=m, bank_u=bank_u: E.matmul(
                                 bank_u[:, 0:512], wgu[:, (m % 2) * 2 + 1, k * 128:(k + 1) * 128],
                                 xnf[:, k, :], start=(k == 0), stop=(k == NK - 1)))
                    lm_gu_mm[m] = w.cnt["mm"]
                    w.wait("scalar", "mm", lm_gu_mm[m])
                    if lm_act_prev is not None:
                        w.wait("scalar", "ev", lm_act_prev)
                    lm_sg = w.op("scalar", "ac", lambda E, m=m, bank_g=bank_g: E.activation(
                        sgt[:, :], bank_g[:, 0:512], AF.Silu))
                    w.wait("vector", "ac", lm_sg)
                    if lm_dn_mm is not None:
                        w.wait("vector", "mm", lm_dn_mm)
                    lm_act = w.op("vector", "ev", lambda E, m=m, bank_u=bank_u: E.tensor_mul(
                        act[:, m, :], sgt[:, :], bank_u[:, 0:512]))
                    lm_act_prev = lm_act
                    lm_gu_ac[m] = (lm_sg, lm_act)
                    lm_act_ev[m] = lm_act
                for f4 in range(4):
                    if lm_dn_mm is not None:
                        w.wait("sync", "mm", lm_dn_mm)
                    t_wd = w.dma("sync", "wld",
                                 wdq[:, f4 % 2, :].rearrange("p (a b) -> p a b", a=8),
                                 wd_h[e, f4, :, :, :].transpose([1, 0, 2]))
                    w.wait("tensor", "wld", t_wd, guard=(ss == 0 and e == 0 and f4 == 0))
                    if f4 == 0:
                        for m in range(8):
                            w.wait("tensor", "ev", lm_act_ev[m])
                    if lm_dn_ev is not None:
                        w.wait("tensor", "ev", lm_dn_ev)
                    for tt4 in range(4):
                        for k8 in range(8):
                            w.op("tensor", "mm" if (tt4 == 3 and k8 == 7) else None,
                                 lambda E, tt4=tt4, k8=k8, f4=f4: E.matmul(
                                     ps[4 + tt4][:, 0:512],
                                     act[:, k8, tt4 * 128:(tt4 + 1) * 128],
                                     wdq[:, f4 % 2, k8 * 512:(k8 + 1) * 512],
                                     start=(k8 == 0), stop=(k8 == 7)))
                    lm_dn_mm = w.cnt["mm"]
                    w.wait("vector", "mm", lm_dn_mm)
                    for tt4 in range(4):
                        tg = ss * 4 + tt4
                        if e == 0:
                            w.op("vector", "ev" if tt4 == 3 else None,
                                 lambda E, tt4=tt4, f4=f4, tg=tg: E.tensor_scalar(
                                     sffn[:, tt4, f4 * 512:(f4 + 1) * 512], ps[4 + tt4][:, 0:512],
                                     w_all[:, tg * 12: tg * 12 + 1], None, op0=AL.mult))
                        else:
                            w.op("vector", "ev" if tt4 == 3 else None,
                                 lambda E, tt4=tt4, f4=f4, tg=tg, e=e: E.scalar_tensor_tensor(
                                     sffn[:, tt4, f4 * 512:(f4 + 1) * 512], ps[4 + tt4][:, 0:512],
                                     w_all[:, tg * 12 + e: tg * 12 + e + 1],
                                     sffn[:, tt4, f4 * 512:(f4 + 1) * 512],
                                     op0=AL.mult, op1=AL.add))
                    lm_dn_ev = w.cnt["ev"]
            for tt4 in range(4):
                tg = ss * 4 + tt4
                if lm_g_w is not None:
                    w.wait("sync", "sd", lm_g_w)
                t_xnti = w.dma("sync", "ld", xnti[:, :], hba["xntm_h"][tg, :, :])
                w.wait("vector", "ld", t_xnti, guard=(ss == 0 and tt4 == 0))
                if lm_g_w is not None:
                    w.wait("vector", "sd", lm_g_w)
                lm_gf = w.op("vector", "ev", lambda E, tt4=tt4, tg=tg: E.scalar_tensor_tensor(
                    gt1[:, :], xnti[:, :], wz_all[:, tg:tg + 1], sffn[:, tt4, :],
                    op0=AL.mult, op1=AL.add))
                w.wait("scalar", "ev", lm_gf)
                lm_gs = w.op("scalar", "ac", lambda E: E.activation(xnti[:, :], gt1[:, :], AF.Silu))
                w.wait("scalar", "ac", lm_gs)
                lm_g_w = w.dma("scalar", "sd", hba["g_h"][tg, :, :], xnti[:, :])
            lm_ss_mm = w.cnt["mm"]
        lm_g_w_all = w.cnt["sd"]

        if upto <= 5:
            dump(["g_h"])
            return _finish(nc, w, (st, xst), sems)

        w.stage_gate()
        st.close()
        st = contextlib.ExitStack()
        S_cur = sbuf("S_cur", [128, 4096])
        mjt = sbuf("mjt", [128, 2, 4104])
        nlg = sbuf("nlg", [16, 32])
        acc_c = sbuf("acc_c", [16, 2])
        rowt = sbuf("rowt", [1, 16])
        coefb = sbuf("coefb", [128, 16])
        lm_s0 = w.op("vector", "ev", lambda E: E.memset(S_cur[:, :], 0.0))
        w.wait("sync", "cc", 1, guard=True)
        t_nl = None
        for l in range(4):
            t_nl = w.dma("sync", "ld", nlg[:, l * 8:(l + 1) * 8], hba["ccout"][l, 0:16, 4096:4104])
        lm_comb = [None] * 4
        for j in range(4):
            if j >= 2:
                w.wait("sync", "ev", lm_comb[j - 2])
            t_mj = w.dma("sync", "ld", mjt[:, j % 2, :], hba["ccout"][j, :, :])
            w.wait("vector", "ld", t_mj, guard=(j == 0))
            if j == 0:
                w.wait("vector", "ld", t_nl)
            lmv6 = w.op("vector", "ev", lambda E: E.memset(acc_c[:, 0:1], 0.0))
            for l in range(4):
                w.wait("vector", "ev", lmv6)
                lmv6 = w.op("vector", "ev", lambda E, j=j, l=l: E.scalar_tensor_tensor(
                    acc_c[:, 0:1], nlg[:, l * 8: l * 8 + 1], small[:, 2 + j * 4 + l: 3 + j * 4 + l],
                    acc_c[:, 0:1], op0=AL.mult, op1=AL.add))
            w.wait("vector", "ev", lmv6)
            lm_acc = w.op("vector", "ev", lambda E: E.tensor_copy(acc_c[:, 1:2], acc_c[:, 0:1]))
            w.wait("scalar", "ev", lm_acc)
            lm_ce = w.op("scalar", "ac", lambda E: E.activation(
                acc_c[:, 0:1], acc_c[:, 1:2], AF.Exp, scale=-1.0))
            w.wait("vector", "ac", lm_ce)
            lm_cu = w.op("vector", "ev", lambda E, j=j: E.tensor_mul(
                acc_c[:, 1:2], acc_c[:, 0:1], small[:, 18 + j: 19 + j]))
            w.wait("tensor", "ev", lm_cu)
            lm_ct = w.op("tensor", "mm", lambda E: E.transpose(
                ps[6][0:1, 0:16], acc_c[:, 1:2], c_ident[0:16, 0:16]))
            w.wait("vector", "mm", lm_ct)
            lm_row = w.op("vector", "ev", lambda E: E.tensor_copy(rowt[:, :], ps[6][0:1, 0:16]))
            w.wait("tensor", "ev", lm_row)
            lm_cb = w.op("tensor", "mm", lambda E: E.matmul(
                ps[3][:, 0:16], c_tri[0:1, 0:128], rowt[0:1, 0:16], start=True, stop=True))
            w.wait("vector", "mm", lm_cb)
            lm_cbc = w.op("vector", "ev", lambda E: E.tensor_copy(coefb[:, :], ps[3][:, 0:16]))
            w.wait("vector", "ev", lm_cbc)
            for h in range(HK):
                lm = w.op("vector", "ev" if h == HK - 1 else None,
                          lambda E, j=j, h=h: E.scalar_tensor_tensor(
                              S_cur[:, h * 256:(h + 1) * 256], mjt[:, j % 2, h * 256:(h + 1) * 256],
                              coefb[:, h:h + 1], S_cur[:, h * 256:(h + 1) * 256],
                              op0=AL.mult, op1=AL.add))
            lm_comb[j] = lm
        lm_sin = lm_comb[3]

        w.wait("tensor", "ev", lm_sin)
        lm_lt = w.op("tensor", "mm", lambda E: E.transpose(ps[7][0:16, 0:16], ltots[:, :], c_ident[0:16, 0:16]))
        ltT = sbuf("ltT", [16, 16])
        w.wait("vector", "mm", lm_lt)
        lm_ltT = w.op("vector", "ev", lambda E: E.tensor_copy(ltT[:, :], ps[7][0:16, 0:16]))

        qht = sbuf("qht", [128, D])
        opt_ = sbuf("opt_", [128, 2 * D])
        mt2 = sbuf("mt2", [128, 2 * D])
        o_all = sbuf("o_all", [128, 2 * D])
        wocol = sbuf("wocol", [128, 2, D])
        s_tm = sbuf("s_tm", [128, D])
        xtile = sbuf("xtile", [128, D])
        gtile = sbuf("gtile", [128, D])
        outt = sbuf("outt", [128, D])
        elt2 = sbuf("elt2", [128, 16])

        lm_p2_cons = None
        lm_out_ev = None
        lm_supd = None
        prev_wo = None
        y_w = []
        lm_wo_mm = [None] * 32
        w.wait("sync", "sd", lm_g_w_all)
        for c in range(1, NCH1):
            ci = c - 1
            if lm_p2_cons is not None:
                w.wait("sync", "mm", lm_p2_cons[0])
                w.wait("sync", "ev", lm_p2_cons[1])
            t_qh = w.dma("sync", "ld", qht[:, :], hba["qhat_h"][ci, :, :])
            t_op = w.dma("sync", "ld", opt_[:, :], hba["opart_h"][ci, :, :])
            t_m2 = w.dma("sync", "ld", mt2[:, :], hba["mc_h"][ci, :, :])
            w.wait("tensor", "ev", lm_ltT if c == 1 else lm_supd)
            lm_pb2 = w.op("tensor", "mm", lambda E, ci=ci: E.matmul(
                ps[2][:, 0:16], ehv[:, ci * 128: ci * 128 + 128], ltT[:, :],
                start=True, stop=True))
            w.wait("scalar", "mm", lm_pb2)
            if lm_supd is not None:
                w.wait("scalar", "ev", lm_supd)
            lm_el2 = w.op("scalar", "ac", lambda E: E.activation(
                elt2[:, :], ps[2][:, 0:16], AF.Exp, scale=-1.0))
            w.wait("tensor", "ld", t_qh, guard=(c == 1))
            if c == 1:
                w.wait("tensor", "ev", lm_sin)
            else:
                w.wait("tensor", "ev", lm_supd)
            lm_o_ev = None
            for h in range(HK):
                if lm_o_ev is not None:
                    w.wait("tensor", "ev", lm_o_ev)
                w.op("tensor", None, lambda E, h=h: E.matmul(
                    ps[0][:, 0:128], S_cur[:, h * 256: h * 256 + 128], qht[:, h * 128:(h + 1) * 128],
                    start=True, stop=True))
                lm_o1 = w.op("tensor", "mm", lambda E, h=h: E.matmul(
                    ps[1][:, 0:128], S_cur[:, h * 256 + 128: h * 256 + 256],
                    qht[:, h * 128:(h + 1) * 128], start=True, stop=True))
                w.wait("vector", "mm", lm_o1)
                if h == 0:
                    w.wait("vector", "ld", t_op, guard=(c == 1))
                    if prev_wo is not None:
                        w.wait("vector", "mm", prev_wo)
                w.op("vector", None, lambda E, h=h: E.tensor_add(
                    o_all[:, h * 256: h * 256 + 128], ps[0][:, 0:128], opt_[:, h * 256: h * 256 + 128]))
                lm_o_ev = w.op("vector", "ev", lambda E, h=h: E.tensor_add(
                    o_all[:, h * 256 + 128: h * 256 + 256], ps[1][:, 0:128],
                    opt_[:, h * 256 + 128: h * 256 + 256]))
            lm_omm = w.cnt["mm"]
            w.wait("vector", "ac", lm_el2)
            w.wait("vector", "ld", t_m2)
            w.wait("vector", "mm", lm_omm)
            for h in range(HK):
                lm_supd = w.op("vector", "ev" if h == HK - 1 else None,
                               lambda E, h=h: E.scalar_tensor_tensor(
                                   S_cur[:, h * 256:(h + 1) * 256], S_cur[:, h * 256:(h + 1) * 256],
                                   elt2[:, h:h + 1], mt2[:, h * 256:(h + 1) * 256],
                                   op0=AL.mult, op1=AL.add))
            for fc in range(32):
                if lm_wo_mm[fc - 2] is not None:
                    w.wait("sync", "mm", lm_wo_mm[fc - 2])
                t_wo = w.dma("sync", "wld", wocol[:, fc % 2, :], wo[fc, :, :])
                w.wait("tensor", "wld", t_wo, guard=(c == 1 and fc == 0))
                if fc == 0:
                    w.wait("tensor", "ev", lm_o_ev)
                    if lm_out_ev is not None:
                        w.wait("tensor", "ev", lm_out_ev)
                for f4 in range(4):
                    w.op("tensor", "mm" if f4 == 3 else None,
                         lambda E, fc=fc, f4=f4: E.matmul(
                             ps[4 + f4][:, 0:512], o_all[:, fc * 128:(fc + 1) * 128],
                             wocol[:, fc % 2, f4 * 512:(f4 + 1) * 512],
                             start=(fc == 0), stop=(fc == 31)))
                lm_wo_mm[fc] = w.cnt["mm"]
            prev_wo = w.cnt["mm"]
            t_x = w.dma("sync", "ld", xtile[:, :], x_sh[c * 128:(c + 1) * 128, :])
            t_g = w.dma("sync", "ld", gtile[:, :], hba["g_h"][ci, :, :])
            w.wait("vector", "mm", prev_wo)
            lm_stm = None
            for f4 in range(4):
                lm_stm = w.op("vector", "ev", lambda E, f4=f4: E.tensor_copy(
                    s_tm[:, f4 * 512:(f4 + 1) * 512], ps[4 + f4][:, 0:512]))
            w.wait("vector", "ld", t_g, guard=(c == 1))
            w.wait("vector", "ld", t_x)
            if y_w:
                w.wait("vector", "gd", y_w[-1])
            w.wait("vector", "ev", lm_stm)
            lmv7 = w.op("vector", "ev", lambda E: E.tensor_mul(outt[:, :], s_tm[:, :], gtile[:, :]))
            w.wait("vector", "ev", lmv7)
            lm_out_ev = w.op("vector", "ev", lambda E: E.tensor_add(outt[:, :], outt[:, :], xtile[:, :]))
            w.wait("scalar", "ev", lm_out_ev)
            y_w.append(w.dma("scalar", "gd", y[ci * 128:(ci + 1) * 128, :], outt[:, :]))
            lm_p2_cons = (w.cnt["mm"], w.cnt["ev"])

        return _finish(nc, w, (st, xst), sems)


def _finish(nc, w, stacks, sems):
    w.wait("gpsimd", "gd", w.cnt.get("gd", 0))
    w.wait("gpsimd", "sd", w.cnt.get("sd", 0))
    w.wait("gpsimd", "ev", w.cnt.get("ev", 0))
    w.wait("gpsimd", "ac", w.cnt.get("ac", 0))
    w.wait("gpsimd", "mm", w.cnt.get("mm", 0))
    w.wait("gpsimd", "ld", w.cnt.get("ld", 0))
    w.wait("gpsimd", "wld", w.cnt.get("wld", 0))

    def _clr(E, ss=tuple(sems.values())):
        for s in ss:
            E.sem_clear(s)
    w.on("gpsimd", _clr)

    with nc.Block() as block:
        for name in ENGINES:
            def body(E, name=name):
                for fn in w.ops[name]:
                    fn(E)
            getattr(block, name)(body)
    for s_ in stacks:
        s_.close()
    return nc


def kernel(**inputs):
    in_maps = stage_inputs(inputs)
    nc = build_program()
    res = run_bass_kernel_spmd(nc, in_maps, core_ids=list(range(8)))
    outs = [res.results[i]["y"] for i in range(8)]
    return np.concatenate(outs, axis=0).reshape(2, 8192, D)


# version 08 (final)
# speedup vs baseline: 1.0009x; 1.0009x over previous
import contextlib
import numpy as np
import concourse.bass as bass
import concourse.mybir as mybir
from concourse.bass_utils import run_bass_kernel_spmd
from concourse import library_config

F32 = mybir.dt.float32
AF = mybir.ActivationFunctionType
AL = mybir.AluOpType
AX = mybir.AxisListType

D = 2048
HK = 16
NK = 16
TOK = 2048
NCH = 16
SH = TOK + 128
NCH1 = 17
GUARD = 8000

ENGINES = ("sync", "tensor", "vector", "scalar", "gpsimd")


class Walk:

    def __init__(self):
        self.ops = {e: [] for e in ENGINES}
        self.cnt = {}
        self.sems = {}
        self.guard = {}

    def on(self, eng, fn):
        self.ops[eng].append(fn)

    def bump(self, sem, n):
        self.cnt[sem] = self.cnt.get(sem, 0) + n
        return self.cnt[sem]

    def wait(self, eng, sem, val, guard=False):
        if val is None or val <= 0:
            return
        self.on(eng, lambda E, s=sem, v=val: E.wait_ge(self.sems[s], v))
        if guard:
            if eng in self.guard:
                dst, src, gs = self.guard[eng]
                tgt = self.bump(gs, 16)
                self.on(eng, lambda E, dst=dst, src=src, gs=gs:
                        E.dma_start(out=dst, in_=src).then_inc(self.sems[gs], 16))
                self.on(eng, lambda E, gs=gs, v=tgt: E.wait_ge(self.sems[gs], v))
            else:
                def _nops(E):
                    for _ in range(160):
                        E.nop()
                self.on(eng, _nops)

    def dma(self, eng, sem, out_ap, in_ap):
        tgt = self.bump(sem, 16)
        self.on(eng, lambda E, o=out_ap, i=in_ap, s=sem:
                E.dma_start(out=o, in_=i).then_inc(self.sems[s], 16))
        return tgt

    def stage_gate(self, engs=("sync", "tensor", "vector", "scalar")):
        snap = dict(self.cnt)
        for e in engs:
            for sem in ("mm", "ev", "ac", "sd", "ld", "wld", "gd"):
                self.wait(e, sem, snap.get(sem, 0))

    def op(self, eng, sem, fn):
        if sem:
            tgt = self.bump(sem, 1)
            self.on(eng, lambda E, f=fn, s=sem: f(E).then_inc(self.sems[s], 1))
            return tgt
        self.on(eng, fn)
        return None


def stage_inputs(inputs):
    i = inputs
    x = np.ascontiguousarray(
        np.asarray(i["L0_attn_hidden_states"], np.float32).reshape(16384, D))
    Wq = np.asarray(i["L0_attn_Wq"], np.float32)
    Wk = np.asarray(i["L0_attn_Wk"], np.float32)
    Wv = np.asarray(i["L0_attn_Wv"], np.float32)
    Wo = np.asarray(i["L0_attn_Wo"], np.float32)
    Wqc = np.asarray(i["L0_attn_Wq_conv"], np.float32)
    bqc = np.asarray(i["L0_attn_bq_conv"], np.float32)
    Wkc = np.asarray(i["L0_attn_Wk_conv"], np.float32)
    bkc = np.asarray(i["L0_attn_bk_conv"], np.float32)
    Wg = np.asarray(i["L0_attn_Wg"], np.float32)
    bg = np.asarray(i["L0_attn_bg"], np.float32)
    Wb = np.asarray(i["L0_attn_Wb"], np.float32)
    bb = np.asarray(i["L0_attn_bb"], np.float32)
    Cw = np.asarray(i["L0_ffn_classifier_w"], np.float32)
    eb = np.asarray(i["L0_ffn_e_score_correction_bias"], np.float32)
    Gw = np.asarray(i["L0_ffn_gate_w"], np.float32)
    Uw = np.asarray(i["L0_ffn_up_w"], np.float32)
    Dw = np.asarray(i["L0_ffn_down_w"], np.float32)

    def tile_A(W):
        M = W.shape[0] // 128
        return np.ascontiguousarray(
            W.reshape(M, 128, NK, 128).transpose(0, 3, 2, 1).reshape(M, 128, NK * 128))

    wqk_t = tile_A(np.concatenate([Wq, Wk], axis=0))
    wv_t = np.ascontiguousarray(
        Wv.reshape(8, 512, NK, 128).transpose(0, 3, 2, 1).reshape(8, 128, NK * 512))
    wo_t = np.ascontiguousarray(Wo.reshape(D, 32, 128).transpose(1, 2, 0))

    def conv_t(W):
        return np.ascontiguousarray(
            W.reshape(HK, 128, 128, 4).transpose(2, 0, 3, 1).reshape(128, HK * 4 * 128))

    wcq_t = conv_t(Wqc)
    wck_t = conv_t(Wkc)
    bconv_t = np.ascontiguousarray(
        np.concatenate([bqc.reshape(HK, 128).T, bkc.reshape(HK, 128).T], axis=1))
    WGB = np.concatenate([Wg, Wb], axis=0)
    wgb_t = np.ascontiguousarray(
        WGB.T.reshape(NK, 128, 32).transpose(1, 0, 2).reshape(128, NK * 32))
    bgbb_t = np.ascontiguousarray(np.stack([bg, bb], axis=1))
    cw_t = np.ascontiguousarray(
        Cw.T.reshape(NK, 128, 12).transpose(1, 0, 2).reshape(128, NK * 12))
    bias12 = np.ascontiguousarray(np.broadcast_to(eb, (128, 12)))

    def tile_Ae(W3):
        return np.ascontiguousarray(
            W3.reshape(8, 8, 128, NK, 128).transpose(0, 1, 4, 3, 2).reshape(8, 8, 128, NK * 128))

    wgf_t = tile_Ae(Gw)
    wuf_t = tile_Ae(Uw)
    wd_t = np.ascontiguousarray(Dw.reshape(8, 4, 512, 8, 128).transpose(0, 1, 3, 4, 2))
    ident = np.eye(128, dtype=np.float32)
    trimask = np.ascontiguousarray(np.triu(np.ones((128, 128), np.float32)))
    eh = np.zeros((16, 16 * 128), np.float32)
    for h in range(16):
        eh[h, h * 128:(h + 1) * 128] = 1.0

    in_maps = []
    for core in range(8):
        start = core * TOK
        xs = np.zeros((SH, D), np.float32)
        if core % 4 == 0:
            xs[128:] = x[start:start + TOK]
        else:
            xs[:] = x[start - 128:start + TOK]
        r = core % 4
        sel = np.zeros((16, 16), np.float32)
        use = np.zeros((16, 4), np.float32)
        for j in range(4):
            if j < r:
                use[:, j] = 1.0
                for l in range(4):
                    if j < l < r:
                        sel[:, j * 4 + l] = 1.0
        selu = np.ascontiguousarray(np.concatenate([sel, use], axis=1))
        in_maps.append({
            "x_sh": np.ascontiguousarray(xs),
            "wqk_t": wqk_t, "wv_t": wv_t, "wo_t": wo_t,
            "wcq_t": wcq_t, "wck_t": wck_t, "bconv_t": bconv_t,
            "wgb_t": wgb_t, "bgbb_t": bgbb_t, "cw_t": cw_t, "bias12": bias12,
            "wgf_t": wgf_t, "wuf_t": wuf_t, "wd_t": wd_t,
            "ident": ident, "trimask": trimask, "selu": selu, "eh_t": eh,
        })
    return in_maps


SCRATCH = {
    "qkpre_h": [32, 128, SH], "xnfm_h": [NK, 128, TOK], "xntm_h": [NCH, 128, D],
    "qfm_h": [HK, 128, TOK], "kfm_h": [HK, 128, TOK], "v_h": [8, TOK, 512],
    "qhat_h": [NCH, 128, D], "opart_h": [NCH, 128, 2 * D], "mc_h": [NCH, 128, 2 * D],
    "g_h": [NCH, 128, D], "ccin": [128, 4104], "ccout": [4, 128, 4104],
}


def build_program(upto=99, debug=()):
    nc = bass.Bass("TRN2", target_bir_lowering=False, detect_race_conditions=False)
    dt = nc.dram_tensor
    x_sh = dt("x_sh", [SH, D], F32, kind="ExternalInput").ap()
    wqk = dt("wqk_t", [32, 128, NK * 128], F32, kind="ExternalInput").ap()
    wv = dt("wv_t", [8, 128, NK * 512], F32, kind="ExternalInput").ap()
    wo = dt("wo_t", [32, 128, D], F32, kind="ExternalInput").ap()
    wcq = dt("wcq_t", [128, HK * 4 * 128], F32, kind="ExternalInput").ap()
    wck = dt("wck_t", [128, HK * 4 * 128], F32, kind="ExternalInput").ap()
    bconv_h = dt("bconv_t", [128, 32], F32, kind="ExternalInput").ap()
    wgb_h = dt("wgb_t", [128, NK * 32], F32, kind="ExternalInput").ap()
    bgbb_h = dt("bgbb_t", [16, 2], F32, kind="ExternalInput").ap()
    cw_h = dt("cw_t", [128, NK * 12], F32, kind="ExternalInput").ap()
    bias12_h = dt("bias12", [128, 12], F32, kind="ExternalInput").ap()
    wgf = dt("wgf_t", [8, 8, 128, NK * 128], F32, kind="ExternalInput").ap()
    wuf = dt("wuf_t", [8, 8, 128, NK * 128], F32, kind="ExternalInput").ap()
    wd_h = dt("wd_t", [8, 4, 8, 128, 512], F32, kind="ExternalInput").ap()
    ident_h = dt("ident", [128, 128], F32, kind="ExternalInput").ap()
    trimask_h = dt("trimask", [128, 128], F32, kind="ExternalInput").ap()
    selu_h = dt("selu", [16, 20], F32, kind="ExternalInput").ap()
    eh_h = dt("eh_t", [16, NK * 128], F32, kind="ExternalInput").ap()
    y = dt("y", [TOK, D], F32, kind="ExternalOutput").ap()

    hb = {n: dt(n, s, F32) for n, s in SCRATCH.items()}
    hba = {n: h.ap() for n, h in hb.items()}
    dbg = {n: dt("dbg_" + n, SCRATCH[n], F32, kind="ExternalOutput").ap() for n in debug}

    w = Walk()
    sem_names = ["go", "ld", "wld", "mm", "ev", "ac", "pb", "sd", "gd", "cc"] + [f"gu_{e}" for e in ENGINES]

    with contextlib.ExitStack() as outer:
        sems = {n: outer.enter_context(nc.semaphore(n)) for n in sem_names}
        w.sems = sems
        ps = [outer.enter_context(nc.psum_tensor(f"ps{i}", [128, 512], F32))
              for i in range(8)]

        def psb(name, shape):
            return outer.enter_context(nc.sbuf_tensor(name, shape, F32))

        consts = psb("consts", [128, 300])
        junk = psb("junk", [128, 5 * 128])
        c_ident = consts[:, 0:128]
        c_tri = consts[:, 128:256]
        c_bconv = consts[:, 256:288]
        c_b12 = consts[:, 288:300]
        small = psb("small", [16, 24])
        gates_all = psb("gates_all", [16, TOK])
        nlb_all = psb("nlb_all", [16, TOK])
        logits_all = psb("logits_all", [128, NCH * 12])
        ltots = psb("ltots", [16, NCH])
        w_all = psb("w_all", [128, NCH * 12])
        wz_all = psb("wz_all", [128, NCH])
        ehv = psb("ehv", [16, NK * 128])
        for ei, e in enumerate(("sync", "scalar")):
            w.guard[e] = (junk[:, ei * 128:(ei + 1) * 128], ident_h, f"gu_{e}")

        def _clear(E, ss=tuple(sems.values()), go=sems["go"]):
            for s in ss:
                E.sem_clear(s)
            return E.sem_inc(go, 1)
        w.on("gpsimd", _clear)
        w.cnt["go"] = 1
        for eng in ("sync", "tensor", "vector", "scalar"):
            w.wait(eng, "go", 1, guard=True)

        w.dma("sync", "ld", consts[:, 0:128], ident_h)
        w.dma("sync", "ld", consts[:, 128:256], trimask_h)
        w.dma("sync", "ld", consts[:, 256:288], bconv_h)
        w.dma("sync", "ld", consts[:, 288:300], bias12_h)
        w.dma("sync", "ld", small[:, 0:2], bgbb_h)
        w.dma("sync", "ld", ehv[:, :], eh_h)
        t_consts = w.dma("sync", "ld", small[:, 2:22], selu_h)
        w.wait("vector", "ld", t_consts, guard=True)
        w.op("vector", "ev", lambda E: E.tensor_scalar_mul(small[:, 22:23], small[:, 1:2], -1.0))
        negbb = small[:, 22:23]

        def dump(names):
            for n in names:
                if n in dbg:
                    w.wait("scalar", "sd", w.cnt.get("sd", 0))
                    w.wait("scalar", "ev", w.cnt.get("ev", 0))
                    w.wait("scalar", "ac", w.cnt.get("ac", 0))
                    w.wait("scalar", "gd", w.cnt.get("gd", 0))
                    if len(SCRATCH[n]) == 3:
                        for i in range(SCRATCH[n][0]):
                            w.dma("scalar", "gd", dbg[n][i], hba[n][i])
                    else:
                        w.dma("scalar", "gd", dbg[n], hba[n])

        st = contextlib.ExitStack()

        def sbuf(name, shape):
            return st.enter_context(nc.sbuf_tensor(name, shape, F32))

        xst = contextlib.ExitStack()
        xnt = xst.enter_context(nc.sbuf_tensor("xnt", [128, NK * SH], F32))
        xt = sbuf("xt", [128, D])
        xnb = sbuf("xnb", [128, 2, D])
        statb = sbuf("statb", [128, 16])

        a_x = [None] * NCH1
        a_xn = [None] * NCH1
        a_sq = [None] * NCH1
        a_trmm = [None] * NCH1
        a_xnw = [None] * NCH1
        tr_copy_lm = []
        for c in range(NCH1):
            if c >= 1:
                w.wait("sync", "ac", a_sq[c - 1])
                w.wait("sync", "ev", a_xn[c - 1])
            a_x[c] = w.dma("sync", "ld", xt[:, :], x_sh[c * 128:(c + 1) * 128, :])

            xa = xt[:, :]
            stc = statb[:, 8 * (c % 2): 8 * (c % 2) + 8]
            w.wait("scalar", "ld", a_x[c], guard=(c <= 1))
            if c >= 2:
                w.wait("scalar", "ev", a_xn[c - 2])
                w.wait("scalar", "mm", a_trmm[c - 2])
                w.wait("scalar", "sd", a_xnw[c - 2])
            a_sq[c] = w.op("scalar", "ac", lambda E, xa=xa, xnc=xnb[:, c % 2, :], stc=stc:
                           E.activation(xnc, xa, AF.Square))
            w.wait("vector", "ld", a_x[c], guard=(c <= 1))
            w.wait("vector", "ac", a_sq[c])
            lm1 = w.op("vector", "ev", lambda E, xnc=xnb[:, c % 2, :], stc=stc: E.reduce_sum(
                stc[:, 1:2], xnc, axis=AX.X))
            lm2 = w.op("vector", "ev", lambda E, xa=xa, stc=stc: E.reduce_sum(stc[:, 0:1], xa, axis=AX.X))
            w.wait("vector", "ev", lm2)
            lm3 = w.op("vector", "ev", lambda E, stc=stc: E.tensor_scalar(
                stc[:, 2:4], stc[:, 0:2], 1.0 / D, None, op0=AL.mult))
            w.wait("vector", "ev", lm3)
            lm4 = w.op("vector", "ev", lambda E, stc=stc: E.tensor_mul(stc[:, 4:5], stc[:, 2:3], stc[:, 2:3]))
            w.wait("vector", "ev", lm4)
            lm5 = w.op("vector", "ev", lambda E, stc=stc: E.tensor_sub(stc[:, 5:6], stc[:, 3:4], stc[:, 4:5]))
            w.wait("vector", "ev", lm5)
            lm6 = w.op("vector", "ev", lambda E, stc=stc: E.tensor_scalar_add(stc[:, 6:7], stc[:, 5:6], 1e-5))
            w.wait("vector", "ev", lm6)
            lm_rc = w.op("vector", "ev", lambda E, stc=stc: E.reciprocal(stc[:, 7:8], stc[:, 6:7]))
            w.wait("scalar", "ev", lm_rc)
            lm_r = w.op("scalar", "ac", lambda E, stc=stc: E.activation(stc[:, 6:7], stc[:, 7:8], AF.Sqrt))
            w.wait("vector", "ac", lm_r)
            w.wait("vector", "ev", lm3)
            if c >= 2:
                w.wait("vector", "mm", a_trmm[c - 2])
                w.wait("vector", "sd", a_xnw[c - 2])
            xnc = xnb[:, c % 2, :]
            a_xn[c] = w.op("vector", "ev", lambda E, xa=xa, xnc=xnc, stc=stc: E.tensor_scalar(
                xnc, xa, stc[:, 2:3], stc[:, 6:7], op0=AL.subtract, op1=AL.mult))
            if c >= 1:
                w.wait("scalar", "ev", a_xn[c])
                a_xnw[c] = w.dma("scalar", "sd", hba["xntm_h"][c - 1, :, :], xnc)
            for k in range(NK):
                bank = ps[6 + (k % 2)]
                w.wait("tensor", "ev", a_xn[c])
                if len(tr_copy_lm) >= 2:
                    w.wait("tensor", "ev", tr_copy_lm[-2])
                lm_t = w.op("tensor", "mm", lambda E, xnc=xnc, k=k, bank=bank: E.transpose(
                    bank[:, 0:128], xnc[:, k * 128:(k + 1) * 128], c_ident))
                w.wait("vector", "mm", lm_t)
                tr_copy_lm.append(w.op("vector", "ev", lambda E, k=k, c=c, bank=bank:
                                       E.tensor_copy(xnt[:, k * SH + c * 128: k * SH + (c + 1) * 128],
                                                     bank[:, 0:128])))
            a_trmm[c] = w.cnt["mm"]
        lmA = dict(w.cnt)

        for k in range(NK):
            w.wait("scalar", "ev", lmA["ev"])
            w.dma("scalar", "sd", hba["xnfm_h"][k, :, :], xnt[:, k * SH + 128:(k + 1) * SH])

        if upto <= 1:
            dump(["xnfm_h", "xntm_h"])
            return _finish(nc, w, (st, xst), sems)

        w.stage_gate()
        st.close()
        st = contextlib.ExitStack()
        wcol = sbuf("wcol", [128, 2, NK * 128])
        qkstage = sbuf("qkstage", [128, SH])
        CB = [(0, 512), (512, 512), (1024, 512), (1536, 512), (2048, 128)]
        lm_qk_mm = [None] * 32
        lm_qk_w = [None] * 32
        grp_ev = []
        for m in range(32):
            if m >= 2:
                w.wait("sync", "mm", lm_qk_mm[m - 2])
            twld = w.dma("sync", "wld", wcol[:, m % 2, :], wqk[m, :, :])
            w.wait("tensor", "wld", twld, guard=(m <= 1))
            if m == 0:
                w.wait("tensor", "ev", lmA["ev"])
            for gi, (o, wd_) in enumerate(CB):
                bank = ps[len(grp_ev) % 2]
                if len(grp_ev) >= 2:
                    w.wait("tensor", "ev", grp_ev[-2])
                lm = None
                for k in range(NK):
                    lm = w.op("tensor", "mm" if k == NK - 1 else None,
                              lambda E, bank=bank, m=m, k=k, o=o, wd_=wd_: E.matmul(
                                  bank[:, 0:wd_], wcol[:, m % 2, k * 128:(k + 1) * 128],
                                  xnt[:, k * SH + o: k * SH + o + wd_],
                                  start=(k == 0), stop=(k == NK - 1)))
                w.wait("vector", "mm", lm)
                if m >= 1 and gi == 0:
                    w.wait("vector", "sd", lm_qk_w[m - 1])
                grp_ev.append(w.op("vector", "ev", lambda E, bank=bank, m=m, o=o, wd_=wd_:
                                   E.tensor_copy(qkstage[:, o:o + wd_], bank[:, 0:wd_])))
            lm_qk_mm[m] = w.cnt["mm"]
            w.wait("scalar", "ev", grp_ev[-1])
            lm_qk_w[m] = w.dma("scalar", "sd", hba["qkpre_h"][m, :, :], qkstage[:, :])
        lm_qkpre_w = w.cnt["sd"]
        if upto == 2.1:
            return _finish(nc, w, (st, xst), sems)

        w.stage_gate()
        st.close()
        st = contextlib.ExitStack()
        wvb = sbuf("wvb", [128, NK * 512])
        vstage = sbuf("vstage", [128, 512])
        lm_v_mm = [None] * 8
        v_ev, v_w = [], []
        for n in range(8):
            if n >= 1:
                w.wait("sync", "mm", lm_v_mm[n - 1])
            twld = w.dma("sync", "wld", wvb[:, :], wv[n, :, :])
            w.wait("tensor", "wld", twld, guard=(n == 0))
            for tt in range(NCH):
                bank = ps[2 + (tt % 2)]
                if len(v_ev) >= 2:
                    w.wait("tensor", "ev", v_ev[-2])
                lm = None
                for k in range(NK):
                    lm = w.op("tensor", "mm" if k == NK - 1 else None,
                              lambda E, bank=bank, k=k, tt=tt: E.matmul(
                                  bank[:, 0:512],
                                  xnt[:, k * SH + 128 + tt * 128: k * SH + 128 + (tt + 1) * 128],
                                  wvb[:, k * 512:(k + 1) * 512],
                                  start=(k == 0), stop=(k == NK - 1)))
                w.wait("vector", "mm", lm)
                if len(v_w) >= 1:
                    w.wait("vector", "sd", v_w[-1])
                v_ev.append(w.op("vector", "ev", lambda E, bank=bank, tt=tt:
                                 E.tensor_copy(vstage[:, :], bank[:, 0:512])))
                w.wait("scalar", "ev", v_ev[-1])
                v_w.append(w.dma("scalar", "sd", hba["v_h"][n, tt * 128:(tt + 1) * 128, :],
                                 vstage[:, :]))
            lm_v_mm[n] = w.cnt["mm"]
        lm_v_w = w.cnt["sd"]
        if upto == 2.2:
            return _finish(nc, w, (st, xst), sems)

        w.stage_gate()
        st.close()
        st = contextlib.ExitStack()
        wgbb = sbuf("wgbb", [128, NK * 32])
        gbtmp = sbuf("gbtmp", [16, 512])
        t_gb = w.dma("sync", "wld", wgbb[:, :], wgb_h)
        gb_ac = []
        w.wait("tensor", "ev", v_ev[-1])
        for cb in range(4):
            bank_g = ps[4 if cb % 2 == 0 else 2]
            bank_b = ps[5 if cb % 2 == 0 else 3]
            w.wait("tensor", "wld", t_gb, guard=(cb == 0))
            if len(gb_ac) >= 2:
                w.wait("tensor", "ac", gb_ac[-2])
            for k in range(NK):
                w.op("tensor", "mm" if k == NK - 1 else None,
                     lambda E, k=k, cb=cb, bank_g=bank_g: E.matmul(
                         bank_g[0:16, 0:512], wgbb[:, k * 32: k * 32 + 16],
                         xnt[:, k * SH + 128 + cb * 512: k * SH + 128 + (cb + 1) * 512],
                         start=(k == 0), stop=(k == NK - 1)))
            lm_b = None
            for k in range(NK):
                lm_b = w.op("tensor", "mm" if k == NK - 1 else None,
                            lambda E, k=k, cb=cb, bank_b=bank_b: E.matmul(
                                bank_b[0:16, 0:512], wgbb[:, k * 32 + 16: k * 32 + 32],
                                xnt[:, k * SH + 128 + cb * 512: k * SH + 128 + (cb + 1) * 512],
                                start=(k == 0), stop=(k == NK - 1)))
            w.wait("scalar", "mm", lm_b)
            w.op("scalar", "ac", lambda E, cb=cb, bank_g=bank_g: E.activation(
                gates_all[:, cb * 512:(cb + 1) * 512], bank_g[0:16, 0:512],
                AF.Sigmoid, bias=small[:, 0:1]))
            lm_sg2 = w.op("scalar", "ac", lambda E, cb=cb, bank_b=bank_b: E.activation(
                gbtmp[:, :], bank_b[0:16, 0:512], AF.Sigmoid, bias=small[:, 1:2]))
            w.wait("scalar", "ac", lm_sg2)
            lm_ln = w.op("scalar", "ac", lambda E: E.activation(
                gbtmp[:, :], gbtmp[:, :], AF.Ln))
            w.wait("vector", "ac", lm_ln)
            lm_ng = w.op("vector", "ev", lambda E, cb=cb: E.tensor_scalar(
                nlb_all[:, cb * 512:(cb + 1) * 512], gbtmp[:, :], -1.0, 80.0,
                op0=AL.mult, op1=AL.min))
            w.wait("scalar", "ev", lm_ng)
            gb_ac.append(w.op("scalar", "ac", lambda E: E.activation(
                gbtmp[0:1, 0:1], gbtmp[0:1, 0:1], AF.Copy)))

        if upto == 2.3:
            return _finish(nc, w, (st, xst), sems)

        cwb = sbuf("cwb", [128, NK * 12])
        t_cw = w.dma("sync", "wld", cwb[:, :], cw_h)
        rt_ev = []
        for tt in range(NCH):
            bank = ps[6 + (tt % 2)]
            w.wait("tensor", "wld", t_cw, guard=(tt == 0))
            if len(rt_ev) >= 2:
                w.wait("tensor", "ev", rt_ev[-2])
            lm = None
            for k in range(NK):
                lm = w.op("tensor", "mm" if k == NK - 1 else None,
                          lambda E, k=k, tt=tt, bank=bank: E.matmul(
                              bank[:, 0:12],
                              xnt[:, k * SH + 128 + tt * 128: k * SH + 128 + (tt + 1) * 128],
                              cwb[:, k * 12:(k + 1) * 12],
                              start=(k == 0), stop=(k == NK - 1)))
            w.wait("vector", "mm", lm)
            rt_ev.append(w.op("vector", "ev", lambda E, tt=tt, bank=bank: E.tensor_copy(
                logits_all[:, tt * 12:(tt + 1) * 12], bank[:, 0:12])))
        lmB1 = dict(w.cnt)

        if upto <= 2:
            dump(["qkpre_h", "v_h", "xnfm_h", "xntm_h"])
            return _finish(nc, w, (st, xst), sems)

        w.stage_gate()
        st.close()
        xst.close()
        st = contextlib.ExitStack()
        wconv = sbuf("wconv", [128, 2, HK * 4 * 128])
        qksl = sbuf("qksl", [128, HK, 515])
        qkout = sbuf("qkout", [128, HK, 512])
        w.wait("sync", "sd", lm_qkpre_w, guard=True)
        w.dma("sync", "wld", wconv[:, 0, :], wcq)
        t_wc = w.dma("sync", "wld", wconv[:, 1, :], wck)
        conv_mm_lm = None
        conv_last_w = None
        conv_ac = []
        for blk in range(4):
            for qk in range(2):
                if conv_mm_lm is not None:
                    w.wait("sync", "mm", conv_mm_lm)
                off = 128 + blk * 512 - 3
                src = hba["qkpre_h"][qk * 16:(qk + 1) * 16, :, off:off + 515]
                w.dma("sync", "ld", qksl[:, 0:8, :], src[0:8].transpose([1, 0, 2]))
                t_sl = w.dma("sync", "ld", qksl[:, 8:16, :], src[8:16].transpose([1, 0, 2]))
                w.wait("tensor", "wld", t_wc, guard=(blk == 0 and qk == 0))
                w.wait("tensor", "ld", t_sl, guard=(blk == 0 and qk == 0))
                if conv_last_w is not None:
                    w.wait("scalar", "sd", conv_last_w)
                for g in range(HK):
                    bank = ps[g % 2]
                    if len(conv_ac) >= 2:
                        w.wait("tensor", "ac", conv_ac[-2])
                    lm = None
                    for t in range(4):
                        lm = w.op("tensor", "mm" if t == 3 else None,
                                  lambda E, g=g, t=t, qk=qk, bank=bank: E.matmul(
                                      bank[:, 0:512],
                                      wconv[:, qk, (g * 4 + t) * 128:(g * 4 + t + 1) * 128],
                                      qksl[:, g, t:t + 512],
                                      start=(t == 0), stop=(t == 3)))
                    w.wait("scalar", "mm", lm)
                    conv_ac.append(w.op("scalar", "ac", lambda E, g=g, qk=qk, bank=bank:
                                        E.activation(qkout[:, g, :], bank[:, 0:512], AF.Silu,
                                                     bias=c_bconv[:, qk * 16 + g: qk * 16 + g + 1])))
                conv_mm_lm = w.cnt["mm"]
                w.wait("scalar", "ac", conv_ac[-1])
                dst = (hba["qfm_h"] if qk == 0 else hba["kfm_h"])[:, :, blk * 512:(blk + 1) * 512]
                conv_last_w = w.dma("scalar", "sd", dst.transpose([1, 0, 2]), qkout[:, :, :])
        lm_conv_w = w.cnt["sd"]

        if upto <= 3:
            dump(["qfm_h", "kfm_h"])
            return _finish(nc, w, (st, xst), sems)

        w.stage_gate()
        st.close()
        st = contextlib.ExitStack()
        msgZ = sbuf("msgZ", [128, 4104])
        qf2 = sbuf("qf2", [128, 2, HK * 128])
        kf2 = sbuf("kf2", [128, 2, HK * 128])
        vt2 = sbuf("vt2", [128, 2, 4096])
        nlt = sbuf("nlt", [16, 128])
        lg = sbuf("lg", [16, 256])
        LT = sbuf("LT", [128, 16])
        rowt4 = sbuf("rowt4", [1, 16])
        revt = sbuf("revt", [128, 16])
        ehat = sbuf("ehat", [128, 16])
        eltb = sbuf("eltb", [128, 16])
        darg = sbuf("darg", [128, 128])
        dtt = sbuf("dtt", [128, 128])
        aTt = sbuf("aTt", [128, 128])
        egt = sbuf("egt", [128, 128])
        eggt = sbuf("eggt", [128, 128])
        gb_sb = sbuf("gb_sb", [128, 128])
        khatt = sbuf("khatt", [128, 128])
        qhatt = sbuf("qhatt", [128, HK * 128])
        opartt = sbuf("opartt", [128, 2 * D])
        mct = sbuf("mct", [128, 2 * D])

        lm_c_cons = None
        lm_c_w = None
        prev_lm_LT = None
        prev_lm_row4 = None
        prev_lm_elt = None
        w.op("vector", "ev", lambda E: E.memset(msgZ[:, :], 0.0))
        w.wait("sync", "sd", lm_conv_w, guard=True)
        w.wait("sync", "sd", lm_v_w)
        lm_cons_hist = [None, None]
        for c in range(1, NCH1):
            cc0 = (c - 1) * 128
            sl = (c - 1) % 2
            qf = qf2[:, sl, :]
            kf = kf2[:, sl, :]
            vt = vt2[:, sl, :]
            if lm_cons_hist[sl] is not None:
                w.wait("sync", "mm", lm_cons_hist[sl][0])
                w.wait("sync", "ev", lm_cons_hist[sl][1])
            w.dma("sync", "ld", qf2[:, sl, :].rearrange("p (g w) -> p g w", g=HK)[:, 0:8, :],
                  hba["qfm_h"][0:8, :, cc0:cc0 + 128].transpose([1, 0, 2]))
            t_qf = w.dma("sync", "ld", qf2[:, sl, :].rearrange("p (g w) -> p g w", g=HK)[:, 8:16, :],
                         hba["qfm_h"][8:16, :, cc0:cc0 + 128].transpose([1, 0, 2]))
            w.dma("sync", "ld", kf2[:, sl, :].rearrange("p (g w) -> p g w", g=HK)[:, 0:8, :],
                  hba["kfm_h"][0:8, :, cc0:cc0 + 128].transpose([1, 0, 2]))
            t_kf = w.dma("sync", "ld", kf2[:, sl, :].rearrange("p (g w) -> p g w", g=HK)[:, 8:16, :],
                         hba["kfm_h"][8:16, :, cc0:cc0 + 128].transpose([1, 0, 2]))
            t_vt = w.dma("sync", "ld", vt2[:, sl, :].rearrange("p (n w) -> p n w", n=8),
                         hba["v_h"][:, cc0:cc0 + 128, :].transpose([1, 0, 2]))

            if c == 1:
                w.wait("vector", "ac", lmB1["ac"])
            if lm_c_w is not None:
                w.wait("vector", "sd", lm_c_w)
            lm_nl = w.op("vector", "ev", lambda E, cc0=cc0: E.tensor_tensor_scan(
                nlt[:, :], nlb_all[:, cc0:cc0 + 128], nlb_all[:, cc0:cc0 + 128],
                0.0, op0=AL.add, op1=AL.bypass))
            w.wait("vector", "ev", lm_nl)
            w.op("vector", None, lambda E: E.tensor_scalar_mul(lg[:, 0:128], nlt[:, :], -1.0))
            w.op("vector", None, lambda E, cc0=cc0: E.tensor_copy(
                lg[:, 128:256], gates_all[:, cc0:cc0 + 128]))
            w.op("vector", None, lambda E, c=c: E.tensor_copy(ltots[:, c - 1:c], nlt[:, 127:128]))
            lm_lg = w.op("vector", "ev", lambda E: E.tensor_add(
                msgZ[0:16, 4096:4097], msgZ[0:16, 4096:4097], nlt[:, 127:128]))
            w.wait("tensor", "ev", lm_lg)
            if prev_lm_LT is not None:
                w.wait("tensor", "ev", prev_lm_LT)
            lm_ltr = w.op("tensor", "mm", lambda E: E.transpose(ps[6][0:128, 0:16], nlt[:, :], c_ident[0:16, 0:16]))
            w.wait("vector", "mm", lm_ltr)
            lm_LT = w.op("vector", "ev", lambda E: E.tensor_copy(LT[:, :], ps[6][0:128, 0:16]))
            w.wait("vector", "ev", lm_LT)
            prev_lm_LT = lm_LT
            if prev_lm_row4 is not None:
                w.wait("tensor", "ev", prev_lm_row4)
            if prev_lm_elt is not None:
                w.wait("tensor", "ac", prev_lm_elt)
            lm_t2 = w.op("tensor", "mm", lambda E: E.transpose(
                ps[7][0:1, 0:16], nlt[:, 127:128], c_ident[0:16, 0:16]))
            w.wait("vector", "mm", lm_t2)
            lm_row4 = w.op("vector", "ev", lambda E: E.tensor_copy(rowt4[:, :], ps[7][0:1, 0:16]))
            prev_lm_row4 = lm_row4
            w.wait("tensor", "ev", lm_row4)
            lm_pb0 = w.op("tensor", "mm", lambda E: E.matmul(
                ps[7][:, 16:32], c_tri[0:1, 0:128], rowt4[0:1, 0:16], start=True, stop=True))
            w.wait("vector", "mm", lm_pb0)
            w.wait("vector", "ev", prev_lm_row4)
            lm_rev = w.op("vector", "ev", lambda E: E.tensor_sub(revt[:, :], LT[:, :], ps[7][:, 16:32]))
            w.wait("scalar", "ev", lm_rev)
            lm_ehat = w.op("scalar", "ac", lambda E: E.activation(ehat[:, :], revt[:, :], AF.Exp))
            lm_elt = w.op("scalar", "ac", lambda E: E.activation(eltb[:, :], ps[7][:, 16:32], AF.Exp, scale=-1.0))
            prev_lm_elt = lm_elt

            lm_aT = None
            lm_op1 = None
            lm_kh = None
            lm_z = None
            lm_eg = None
            for h in (range(0) if upto == 4.1 else range(HK)):
                if lm_op1 is not None:
                    w.wait("tensor", "ev", lm_op1)
                if lm_eg is not None:
                    w.wait("tensor", "ac", lm_eg)
                lm_bc = w.op("tensor", "mm", lambda E, h=h: E.matmul(
                    ps[5][:, 0:256], ehv[:, h * 128:(h + 1) * 128], lg[:, 0:256],
                    start=True, stop=True))
                if h == 0:
                    w.wait("tensor", "ld", t_qf, guard=(c == 1))
                    w.wait("tensor", "ld", t_kf)
                    w.wait("tensor", "ld", t_vt)
                if lm_aT is not None:
                    w.wait("tensor", "ev", lm_aT)
                lm_a = w.op("tensor", "mm", lambda E, h=h, kf=kf, qf=qf: E.matmul(
                    ps[0][:, 0:128], kf[:, h * 128:(h + 1) * 128], qf[:, h * 128:(h + 1) * 128],
                    start=True, stop=True))
                w.wait("vector", "mm", lm_bc)
                lm_gbc = w.op("vector", "ev", lambda E: E.tensor_copy(gb_sb[:, :], ps[5][:, 128:256]))
                lm_darg = w.op("vector", "ev", lambda E, h=h: E.tensor_scalar(
                    darg[:, :], ps[5][:, 0:128], LT[:, h:h + 1], 0.0, op0=AL.add, op1=AL.min))
                w.wait("scalar", "ac", lm_elt)
                w.wait("scalar", "ev", lm_darg)
                lm_dt = w.op("scalar", "ac", lambda E: E.activation(dtt[:, :], darg[:, :], AF.Exp))
                w.wait("vector", "ac", lm_dt)
                lm_dm = w.op("vector", "ev", lambda E: E.tensor_mul(dtt[:, :], dtt[:, :], c_tri))
                w.wait("vector", "mm", lm_a)
                w.wait("vector", "ev", lm_dm)
                lm_aT = w.op("vector", "ev", lambda E: E.tensor_mul(aTt[:, :], ps[0][:, 0:128], dtt[:, :]))
                w.wait("tensor", "ev", lm_aT)
                if lm_op1 is not None:
                    w.wait("tensor", "ev", lm_op1)
                w.op("tensor", None, lambda E, h=h, vt=vt: E.matmul(
                    ps[1][:, 0:128], vt[:, h * 256: h * 256 + 128], aTt[:, :],
                    start=True, stop=True))
                lm_oi1 = w.op("tensor", "mm", lambda E, h=h, vt=vt: E.matmul(
                    ps[4][:, 0:128], vt[:, h * 256 + 128: h * 256 + 256], aTt[:, :],
                    start=True, stop=True))
                w.wait("scalar", "mm", lm_bc)
                lm_eg = w.op("scalar", "ac", lambda E: E.activation(egt[:, :], ps[5][:, 0:128], AF.Exp))
                w.wait("vector", "ac", lm_eg)
                w.wait("vector", "ev", lm_gbc)
                lm_egg = w.op("vector", "ev", lambda E: E.tensor_mul(eggt[:, :], egt[:, :], gb_sb[:, :]))
                w.wait("vector", "ev", lm_egg)
                lm_qh = w.op("vector", "ev", lambda E, h=h, qf=qf: E.tensor_mul(
                    qhatt[:, h * 128:(h + 1) * 128], qf[:, h * 128:(h + 1) * 128], eggt[:, :]))
                w.wait("vector", "mm", lm_oi1)
                w.wait("vector", "ev", lm_gbc)
                w.op("vector", None, lambda E, h=h: E.tensor_tensor(
                    opartt[:, h * 256: h * 256 + 128], ps[1][:, 0:128], gb_sb[:, :], op=AL.mult))
                lm_op1 = w.op("vector", "ev", lambda E, h=h: E.tensor_tensor(
                    opartt[:, h * 256 + 128: h * 256 + 256], ps[4][:, 0:128], gb_sb[:, :],
                    op=AL.mult))
                if upto == 4.2:
                    lm_z = lm_op1
                    continue
                if lm_kh is not None:
                    w.wait("tensor", "ac", lm_kh)
                lm_kt = w.op("tensor", "mm", lambda E, h=h, kf=kf: E.transpose(
                    ps[2][:, 0:128], kf[:, h * 128:(h + 1) * 128], c_ident))
                w.wait("scalar", "mm", lm_kt)
                if h == 0:
                    w.wait("scalar", "ac", lm_ehat)
                lm_kh = w.op("scalar", "ac", lambda E, h=h: E.activation(
                    khatt[:, :], ps[2][:, 0:128], AF.Copy, scale=ehat[:, h:h + 1]))
                w.wait("tensor", "ac", lm_kh)
                if lm_z is not None:
                    w.wait("tensor", "ev", lm_z)
                lm_s = w.op("tensor", "mm", lambda E, h=h, vt=vt: E.matmul(
                    ps[3][:, 0:256], khatt[:, :], vt[:, h * 256:(h + 1) * 256],
                    start=True, stop=True))
                w.wait("vector", "mm", lm_s)
                if h == 0:
                    w.wait("vector", "ac", lm_elt)
                w.op("vector", None, lambda E, h=h: E.tensor_copy(
                    mct[:, h * 256:(h + 1) * 256], ps[3][:, 0:256]))
                lm_z = w.op("vector", "ev", lambda E, h=h: E.scalar_tensor_tensor(
                    msgZ[:, h * 256:(h + 1) * 256], msgZ[:, h * 256:(h + 1) * 256],
                    eltb[:, h:h + 1], ps[3][:, 0:256], op0=AL.mult, op1=AL.add))
            lm_c_cons = (w.cnt["mm"], w.cnt["ev"])
            lm_cons_hist[sl] = lm_c_cons
            if upto == 4.1:
                continue
            w.wait("scalar", "ev", lm_z)
            w.dma("scalar", "sd", hba["qhat_h"][c - 1, :, :], qhatt[:, :])
            w.dma("scalar", "sd", hba["opart_h"][c - 1, :, :], opartt[:, :])
            lm_c_w = w.dma("scalar", "sd", hba["mc_h"][c - 1, :, :], mct[:, :])

        if upto in (4.1, 4.2, 4.3):
            return _finish(nc, w, (st, xst), sems)
        w.wait("scalar", "ev", w.cnt["ev"])
        t_ccin = w.dma("scalar", "gd", hba["ccin"][:, :], msgZ[:, :])
        w.wait("gpsimd", "gd", t_ccin)
        if upto == 4.4:
            return _finish(nc, w, (st, xst), sems)
        w.op("gpsimd", "cc", lambda E: E.collective_compute(
            "AllGather", AL.bypass, replica_groups=[[0, 1, 2, 3], [4, 5, 6, 7]],
            ins=[hb["ccin"].ap().opt()], outs=[hb["ccout"].ap().opt()]))

        if upto <= 4:
            dump(["qhat_h", "opart_h", "mc_h", "ccout"])
            return _finish(nc, w, (st, xst), sems)

        w.stage_gate()
        st.close()
        st = contextlib.ExitStack()
        rb = sbuf("rb", [128, 8])
        probs_t = sbuf("probs_t", [128, 12])
        bi_t = sbuf("bi_t", [128, 12])
        m1t = sbuf("m1t", [128, 12])
        m2t = sbuf("m2t", [128, 12])
        b2t = sbuf("b2t", [128, 12])
        for tt in range(NCH):
            sl = logits_all[:, tt * 12:(tt + 1) * 12]
            lm_rm = w.op("vector", "ev", lambda E, sl=sl: E.reduce_max(
                rb[:, 0:1], sl, axis=AX.X, negate=True))
            w.wait("scalar", "ev", lm_rm)
            lm_e = w.op("scalar", "ac", lambda E, sl=sl: E.activation(
                probs_t[:, :], sl, AF.Exp, bias=rb[:, 0:1]))
            w.wait("vector", "ac", lm_e)
            lmv = w.op("vector", "ev", lambda E: E.reduce_sum(rb[:, 1:2], probs_t[:, :], axis=AX.X))
            w.wait("vector", "ev", lmv)
            lmv = w.op("vector", "ev", lambda E: E.reciprocal(rb[:, 2:3], rb[:, 1:2]))
            w.wait("vector", "ev", lmv)
            lmv = w.op("vector", "ev", lambda E: E.tensor_scalar_mul(probs_t[:, :], probs_t[:, :], rb[:, 2:3]))
            w.wait("vector", "ev", lmv)
            lmv = w.op("vector", "ev", lambda E: E.tensor_add(bi_t[:, :], probs_t[:, :], c_b12))
            w.wait("vector", "ev", lmv)
            lmv = w.op("vector", "ev", lambda E: E.reduce_max(rb[:, 3:4], bi_t[:, :], axis=AX.X))
            w.wait("vector", "ev", lmv)
            lmv = w.op("vector", "ev", lambda E: E.tensor_scalar(
                m1t[:, :], bi_t[:, :], rb[:, 3:4], None, op0=AL.is_ge))
            w.wait("vector", "ev", lmv)
            lmv = w.op("vector", "ev", lambda E: E.scalar_tensor_tensor(
                b2t[:, :], m1t[:, :], -1e30, bi_t[:, :], op0=AL.mult, op1=AL.add))
            w.wait("vector", "ev", lmv)
            lmv = w.op("vector", "ev", lambda E: E.reduce_max(rb[:, 4:5], b2t[:, :], axis=AX.X))
            w.wait("vector", "ev", lmv)
            lmv = w.op("vector", "ev", lambda E: E.tensor_scalar(
                m2t[:, :], b2t[:, :], rb[:, 4:5], None, op0=AL.is_ge))
            w.wait("vector", "ev", lmv)
            lmv = w.op("vector", "ev", lambda E: E.tensor_add(m1t[:, :], m1t[:, :], m2t[:, :]))
            w.wait("vector", "ev", lmv)
            lmv = w.op("vector", "ev", lambda E: E.tensor_mul(m1t[:, :], m1t[:, :], probs_t[:, :]))
            w.wait("vector", "ev", lmv)
            lmv = w.op("vector", "ev", lambda E, tt=tt: E.tensor_scalar_mul(
                w_all[:, tt * 12:(tt + 1) * 12], m1t[:, :], 1.5))
            w.wait("vector", "ev", lmv)
            w.op("vector", "ev", lambda E, tt=tt: E.reduce_sum(
                wz_all[:, tt:tt + 1], w_all[:, tt * 12 + 8: tt * 12 + 12], axis=AX.X))

        xnf = sbuf("xnf", [128, NK, 512])
        wgu = sbuf("wgu", [128, 4, NK * 128])
        sgt = sbuf("sgt", [128, 512])
        act = sbuf("act", [128, 8, 512])
        wdq = sbuf("wdq", [128, 2, 8 * 512])
        sffn = sbuf("sffn", [128, 4, D])
        gt1 = sbuf("gt1", [128, D])
        xnti = sbuf("xnti", [128, D])

        lm_ss_mm = None
        lm_gu_mm = [None] * 8
        lm_act_ev = [None] * 8
        lm_gu_ac = [None] * 8
        lm_dn_mm = None
        lm_dn_ev = None
        lm_g_w = None
        lm_act_prev = None
        for ss in range(4):
            if lm_ss_mm is not None:
                w.wait("sync", "mm", lm_ss_mm)
            t_xnf = None
            for k in range(NK):
                t_xnf = w.dma("sync", "ld", xnf[:, k, :],
                              hba["xnfm_h"][k, :, ss * 512:(ss + 1) * 512])
            for e in range(8):
                for m in range(8):
                    if m >= 2:
                        w.wait("sync", "mm", lm_gu_mm[m - 2])
                    w.dma("sync", "wld", wgu[:, (m % 2) * 2, :], wgf[e, m, :, :])
                    t_wgu = w.dma("sync", "wld", wgu[:, (m % 2) * 2 + 1, :], wuf[e, m, :, :])
                    bank_g = ps[0 + (m % 2)]
                    bank_u = ps[2 + (m % 2)]
                    w.wait("tensor", "wld", t_wgu, guard=(ss == 0 and e == 0 and m == 0))
                    if e == 0 and m == 0:
                        w.wait("tensor", "ld", t_xnf)
                    if lm_gu_ac[m] is not None:
                        w.wait("tensor", "ac", lm_gu_ac[m][0])
                        w.wait("tensor", "ev", lm_gu_ac[m][1])
                    for k in range(NK):
                        w.op("tensor", None, lambda E, k=k, m=m, bank_g=bank_g: E.matmul(
                            bank_g[:, 0:512], wgu[:, (m % 2) * 2, k * 128:(k + 1) * 128],
                            xnf[:, k, :], start=(k == 0), stop=(k == NK - 1)))
                    for k in range(NK):
                        w.op("tensor", "mm" if k == NK - 1 else None,
                             lambda E, k=k, m=m, bank_u=bank_u: E.matmul(
                                 bank_u[:, 0:512], wgu[:, (m % 2) * 2 + 1, k * 128:(k + 1) * 128],
                                 xnf[:, k, :], start=(k == 0), stop=(k == NK - 1)))
                    lm_gu_mm[m] = w.cnt["mm"]
                    w.wait("scalar", "mm", lm_gu_mm[m])
                    if lm_act_prev is not None:
                        w.wait("scalar", "ev", lm_act_prev)
                    lm_sg = w.op("scalar", "ac", lambda E, m=m, bank_g=bank_g: E.activation(
                        sgt[:, :], bank_g[:, 0:512], AF.Silu))
                    w.wait("vector", "ac", lm_sg)
                    if lm_dn_mm is not None:
                        w.wait("vector", "mm", lm_dn_mm)
                    lm_act = w.op("vector", "ev", lambda E, m=m, bank_u=bank_u: E.tensor_mul(
                        act[:, m, :], sgt[:, :], bank_u[:, 0:512]))
                    lm_act_prev = lm_act
                    lm_gu_ac[m] = (lm_sg, lm_act)
                    lm_act_ev[m] = lm_act
                for f4 in range(4):
                    if lm_dn_mm is not None:
                        w.wait("sync", "mm", lm_dn_mm)
                    t_wd = w.dma("sync", "wld",
                                 wdq[:, f4 % 2, :].rearrange("p (a b) -> p a b", a=8),
                                 wd_h[e, f4, :, :, :].transpose([1, 0, 2]))
                    w.wait("tensor", "wld", t_wd, guard=(ss == 0 and e == 0 and f4 == 0))
                    if f4 == 0:
                        for m in range(8):
                            w.wait("tensor", "ev", lm_act_ev[m])
                    if lm_dn_ev is not None:
                        w.wait("tensor", "ev", lm_dn_ev)
                    for tt4 in range(4):
                        for k8 in range(8):
                            w.op("tensor", "mm" if (tt4 == 3 and k8 == 7) else None,
                                 lambda E, tt4=tt4, k8=k8, f4=f4: E.matmul(
                                     ps[4 + tt4][:, 0:512],
                                     act[:, k8, tt4 * 128:(tt4 + 1) * 128],
                                     wdq[:, f4 % 2, k8 * 512:(k8 + 1) * 512],
                                     start=(k8 == 0), stop=(k8 == 7)))
                    lm_dn_mm = w.cnt["mm"]
                    w.wait("vector", "mm", lm_dn_mm)
                    for tt4 in range(4):
                        tg = ss * 4 + tt4
                        if e == 0:
                            w.op("vector", "ev" if tt4 == 3 else None,
                                 lambda E, tt4=tt4, f4=f4, tg=tg: E.tensor_scalar(
                                     sffn[:, tt4, f4 * 512:(f4 + 1) * 512], ps[4 + tt4][:, 0:512],
                                     w_all[:, tg * 12: tg * 12 + 1], None, op0=AL.mult))
                        else:
                            w.op("vector", "ev" if tt4 == 3 else None,
                                 lambda E, tt4=tt4, f4=f4, tg=tg, e=e: E.scalar_tensor_tensor(
                                     sffn[:, tt4, f4 * 512:(f4 + 1) * 512], ps[4 + tt4][:, 0:512],
                                     w_all[:, tg * 12 + e: tg * 12 + e + 1],
                                     sffn[:, tt4, f4 * 512:(f4 + 1) * 512],
                                     op0=AL.mult, op1=AL.add))
                    lm_dn_ev = w.cnt["ev"]
            for tt4 in range(4):
                tg = ss * 4 + tt4
                if lm_g_w is not None:
                    w.wait("sync", "sd", lm_g_w)
                t_xnti = w.dma("sync", "ld", xnti[:, :], hba["xntm_h"][tg, :, :])
                w.wait("vector", "ld", t_xnti, guard=(ss == 0 and tt4 == 0))
                if lm_g_w is not None:
                    w.wait("vector", "sd", lm_g_w)
                lm_gf = w.op("vector", "ev", lambda E, tt4=tt4, tg=tg: E.scalar_tensor_tensor(
                    gt1[:, :], xnti[:, :], wz_all[:, tg:tg + 1], sffn[:, tt4, :],
                    op0=AL.mult, op1=AL.add))
                w.wait("scalar", "ev", lm_gf)
                lm_gs = w.op("scalar", "ac", lambda E: E.activation(xnti[:, :], gt1[:, :], AF.Silu))
                w.wait("scalar", "ac", lm_gs)
                lm_g_w = w.dma("scalar", "sd", hba["g_h"][tg, :, :], xnti[:, :])
            lm_ss_mm = w.cnt["mm"]
        lm_g_w_all = w.cnt["sd"]

        if upto <= 5:
            dump(["g_h"])
            return _finish(nc, w, (st, xst), sems)

        w.stage_gate()
        st.close()
        st = contextlib.ExitStack()
        S_cur = sbuf("S_cur", [128, 4096])
        mjt = sbuf("mjt", [128, 2, 4104])
        nlg = sbuf("nlg", [16, 32])
        acc_c = sbuf("acc_c", [16, 2])
        rowt = sbuf("rowt", [1, 16])
        coefb = sbuf("coefb", [128, 16])
        lm_s0 = w.op("vector", "ev", lambda E: E.memset(S_cur[:, :], 0.0))
        w.wait("sync", "cc", 1, guard=True)
        t_nl = None
        for l in range(4):
            t_nl = w.dma("sync", "ld", nlg[:, l * 8:(l + 1) * 8], hba["ccout"][l, 0:16, 4096:4104])
        lm_comb = [None] * 4
        for j in range(4):
            if j >= 2:
                w.wait("sync", "ev", lm_comb[j - 2])
            t_mj = w.dma("sync", "ld", mjt[:, j % 2, :], hba["ccout"][j, :, :])
            w.wait("vector", "ld", t_mj, guard=(j == 0))
            if j == 0:
                w.wait("vector", "ld", t_nl)
            lmv6 = w.op("vector", "ev", lambda E: E.memset(acc_c[:, 0:1], 0.0))
            for l in range(4):
                w.wait("vector", "ev", lmv6)
                lmv6 = w.op("vector", "ev", lambda E, j=j, l=l: E.scalar_tensor_tensor(
                    acc_c[:, 0:1], nlg[:, l * 8: l * 8 + 1], small[:, 2 + j * 4 + l: 3 + j * 4 + l],
                    acc_c[:, 0:1], op0=AL.mult, op1=AL.add))
            w.wait("vector", "ev", lmv6)
            lm_acc = w.op("vector", "ev", lambda E: E.tensor_copy(acc_c[:, 1:2], acc_c[:, 0:1]))
            w.wait("scalar", "ev", lm_acc)
            lm_ce = w.op("scalar", "ac", lambda E: E.activation(
                acc_c[:, 0:1], acc_c[:, 1:2], AF.Exp, scale=-1.0))
            w.wait("vector", "ac", lm_ce)
            lm_cu = w.op("vector", "ev", lambda E, j=j: E.tensor_mul(
                acc_c[:, 1:2], acc_c[:, 0:1], small[:, 18 + j: 19 + j]))
            w.wait("tensor", "ev", lm_cu)
            lm_ct = w.op("tensor", "mm", lambda E: E.transpose(
                ps[6][0:1, 0:16], acc_c[:, 1:2], c_ident[0:16, 0:16]))
            w.wait("vector", "mm", lm_ct)
            lm_row = w.op("vector", "ev", lambda E: E.tensor_copy(rowt[:, :], ps[6][0:1, 0:16]))
            w.wait("tensor", "ev", lm_row)
            lm_cb = w.op("tensor", "mm", lambda E: E.matmul(
                ps[3][:, 0:16], c_tri[0:1, 0:128], rowt[0:1, 0:16], start=True, stop=True))
            w.wait("vector", "mm", lm_cb)
            lm_cbc = w.op("vector", "ev", lambda E: E.tensor_copy(coefb[:, :], ps[3][:, 0:16]))
            w.wait("vector", "ev", lm_cbc)
            for h in range(HK):
                lm = w.op("vector", "ev" if h == HK - 1 else None,
                          lambda E, j=j, h=h: E.scalar_tensor_tensor(
                              S_cur[:, h * 256:(h + 1) * 256], mjt[:, j % 2, h * 256:(h + 1) * 256],
                              coefb[:, h:h + 1], S_cur[:, h * 256:(h + 1) * 256],
                              op0=AL.mult, op1=AL.add))
            lm_comb[j] = lm
        lm_sin = lm_comb[3]

        w.wait("tensor", "ev", lm_sin)
        lm_lt = w.op("tensor", "mm", lambda E: E.transpose(ps[7][0:16, 0:16], ltots[:, :], c_ident[0:16, 0:16]))
        ltT = sbuf("ltT", [16, 16])
        w.wait("vector", "mm", lm_lt)
        lm_ltT = w.op("vector", "ev", lambda E: E.tensor_copy(ltT[:, :], ps[7][0:16, 0:16]))

        qht = sbuf("qht", [128, D])
        opt_ = sbuf("opt_", [128, 2 * D])
        mt2 = sbuf("mt2", [128, 2 * D])
        o_all = sbuf("o_all", [128, 2 * D])
        wocol = sbuf("wocol", [128, 2, D])
        s_tm = sbuf("s_tm", [128, D])
        xtile = sbuf("xtile", [128, D])
        gtile = sbuf("gtile", [128, D])
        outt = sbuf("outt", [128, D])
        elt2 = sbuf("elt2", [128, 16])

        lm_p2_cons = None
        lm_out_ev = None
        lm_supd = None
        prev_wo = None
        y_w = []
        lm_wo_mm = [None] * 32
        w.wait("sync", "sd", lm_g_w_all)
        for c in range(1, NCH1):
            ci = c - 1
            if lm_p2_cons is not None:
                w.wait("sync", "mm", lm_p2_cons[0])
                w.wait("sync", "ev", lm_p2_cons[1])
            t_qh = w.dma("sync", "ld", qht[:, :], hba["qhat_h"][ci, :, :])
            t_op = w.dma("sync", "ld", opt_[:, :], hba["opart_h"][ci, :, :])
            t_m2 = w.dma("sync", "ld", mt2[:, :], hba["mc_h"][ci, :, :])
            w.wait("tensor", "ev", lm_ltT if c == 1 else lm_supd)
            lm_pb2 = w.op("tensor", "mm", lambda E, ci=ci: E.matmul(
                ps[2][:, 0:16], ehv[:, ci * 128: ci * 128 + 128], ltT[:, :],
                start=True, stop=True))
            w.wait("scalar", "mm", lm_pb2)
            if lm_supd is not None:
                w.wait("scalar", "ev", lm_supd)
            lm_el2 = w.op("scalar", "ac", lambda E: E.activation(
                elt2[:, :], ps[2][:, 0:16], AF.Exp, scale=-1.0))
            w.wait("tensor", "ld", t_qh, guard=(c == 1))
            if c == 1:
                w.wait("tensor", "ev", lm_sin)
            else:
                w.wait("tensor", "ev", lm_supd)
            lm_o_ev = None
            for h in range(HK):
                if lm_o_ev is not None:
                    w.wait("tensor", "ev", lm_o_ev)
                w.op("tensor", None, lambda E, h=h: E.matmul(
                    ps[0][:, 0:128], S_cur[:, h * 256: h * 256 + 128], qht[:, h * 128:(h + 1) * 128],
                    start=True, stop=True))
                lm_o1 = w.op("tensor", "mm", lambda E, h=h: E.matmul(
                    ps[1][:, 0:128], S_cur[:, h * 256 + 128: h * 256 + 256],
                    qht[:, h * 128:(h + 1) * 128], start=True, stop=True))
                w.wait("vector", "mm", lm_o1)
                if h == 0:
                    w.wait("vector", "ld", t_op, guard=(c == 1))
                    if prev_wo is not None:
                        w.wait("vector", "mm", prev_wo)
                w.op("vector", None, lambda E, h=h: E.tensor_add(
                    o_all[:, h * 256: h * 256 + 128], ps[0][:, 0:128], opt_[:, h * 256: h * 256 + 128]))
                lm_o_ev = w.op("vector", "ev", lambda E, h=h: E.tensor_add(
                    o_all[:, h * 256 + 128: h * 256 + 256], ps[1][:, 0:128],
                    opt_[:, h * 256 + 128: h * 256 + 256]))
            lm_omm = w.cnt["mm"]
            w.wait("vector", "ac", lm_el2)
            w.wait("vector", "ld", t_m2)
            w.wait("vector", "mm", lm_omm)
            for h in range(HK):
                lm_supd = w.op("vector", "ev" if h == HK - 1 else None,
                               lambda E, h=h: E.scalar_tensor_tensor(
                                   S_cur[:, h * 256:(h + 1) * 256], S_cur[:, h * 256:(h + 1) * 256],
                                   elt2[:, h:h + 1], mt2[:, h * 256:(h + 1) * 256],
                                   op0=AL.mult, op1=AL.add))
            for fc in range(32):
                if lm_wo_mm[fc - 2] is not None:
                    w.wait("sync", "mm", lm_wo_mm[fc - 2])
                t_wo = w.dma("sync", "wld", wocol[:, fc % 2, :], wo[fc, :, :])
                w.wait("tensor", "wld", t_wo, guard=(c == 1 and fc == 0))
                if fc == 0:
                    w.wait("tensor", "ev", lm_o_ev)
                    if lm_out_ev is not None:
                        w.wait("tensor", "ev", lm_out_ev)
                for f4 in range(4):
                    w.op("tensor", "mm" if f4 == 3 else None,
                         lambda E, fc=fc, f4=f4: E.matmul(
                             ps[4 + f4][:, 0:512], o_all[:, fc * 128:(fc + 1) * 128],
                             wocol[:, fc % 2, f4 * 512:(f4 + 1) * 512],
                             start=(fc == 0), stop=(fc == 31)))
                lm_wo_mm[fc] = w.cnt["mm"]
            prev_wo = w.cnt["mm"]
            t_x = w.dma("sync", "ld", xtile[:, :], x_sh[c * 128:(c + 1) * 128, :])
            t_g = w.dma("sync", "ld", gtile[:, :], hba["g_h"][ci, :, :])
            w.wait("vector", "mm", prev_wo)
            lm_stm = None
            for f4 in range(4):
                lm_stm = w.op("vector", "ev", lambda E, f4=f4: E.tensor_copy(
                    s_tm[:, f4 * 512:(f4 + 1) * 512], ps[4 + f4][:, 0:512]))
            w.wait("vector", "ld", t_g, guard=(c == 1))
            w.wait("vector", "ld", t_x)
            if y_w:
                w.wait("vector", "gd", y_w[-1])
            w.wait("vector", "ev", lm_stm)
            lmv7 = w.op("vector", "ev", lambda E: E.tensor_mul(outt[:, :], s_tm[:, :], gtile[:, :]))
            w.wait("vector", "ev", lmv7)
            lm_out_ev = w.op("vector", "ev", lambda E: E.tensor_add(outt[:, :], outt[:, :], xtile[:, :]))
            w.wait("scalar", "ev", lm_out_ev)
            y_w.append(w.dma("scalar", "gd", y[ci * 128:(ci + 1) * 128, :], outt[:, :]))
            lm_p2_cons = (w.cnt["mm"], w.cnt["ev"])

        return _finish(nc, w, (st, xst), sems)


def _finish(nc, w, stacks, sems):
    w.wait("gpsimd", "gd", w.cnt.get("gd", 0))
    w.wait("gpsimd", "sd", w.cnt.get("sd", 0))
    w.wait("gpsimd", "ev", w.cnt.get("ev", 0))
    w.wait("gpsimd", "ac", w.cnt.get("ac", 0))
    w.wait("gpsimd", "mm", w.cnt.get("mm", 0))
    w.wait("gpsimd", "ld", w.cnt.get("ld", 0))
    w.wait("gpsimd", "wld", w.cnt.get("wld", 0))

    def _clr(E, ss=tuple(sems.values())):
        for s in ss:
            E.sem_clear(s)
    w.on("gpsimd", _clr)

    with nc.Block() as block:
        for name in ENGINES:
            def body(E, name=name):
                for fn in w.ops[name]:
                    fn(E)
            getattr(block, name)(body)
    for s_ in stacks:
        s_.close()
    return nc


def kernel(**inputs):
    in_maps = stage_inputs(inputs)
    nc = build_program()
    res = run_bass_kernel_spmd(nc, in_maps, core_ids=list(range(8)))
    outs = [res.results[i]["y"] for i in range(8)]
    return np.concatenate(outs, axis=0).reshape(2, 8192, D)
